# version 11
# baseline (speedup 1.0000x reference)
import contextlib
import numpy as np
import concourse.bass as bass
import concourse.mybir as mybir
from concourse.bass_utils import run_bass_kernel_spmd
import ml_dtypes

F32 = mybir.dt.float32
BF16 = mybir.dt.bfloat16
I32 = mybir.dt.int32
AF = mybir.ActivationFunctionType
ALU = mybir.AluOpType
AX = mybir.AxisListType

L = 2048
D = 2048
HID = 5632
NHB = HID // 128
NDB = D // 128
NTT = L // 128
CH = 512
NCH = L // CH
TCH = CH // 128
K_, KQ, M_, H_ = 32, 4, 8, 8
MEM = K_ * H_
ZC = MEM + K_
QD = KQ * H_ * M_ * 2
PI = float(np.pi)
MAGIC = 12582912.0
INV2PI = float(1.0 / (2 * PI))
TWOPI = float(2 * PI)
PI_CLAMP = 3.14159
EPS_LN = 1e-5

CONST_VALUES = [-128.0, 1.0 / D, EPS_LN, 0.5, -0.5, PI / 2, -PI / 2,
                20.0, -20.0, MAGIC, -MAGIC, 1e-4, -1.0, 2.0, 1.0]


class Ctr:
    def __init__(self, h):
        self.h = h
        self.n = 0

    def inc(self, ins, k=1):
        ins.then_inc(self.h, k)
        self.n += k
        return self.n


class DmaPair:
    def __init__(self, c0, c1):
        self.c = [c0, c1]

    def inc(self, i, ins):
        self.c[i % 2].inc(ins, 16)

    def wait(self, eng, i):
        eng.wait_ge(self.c[i % 2].h, 16 * (i // 2 + 1))

    def wait_all(self, eng):
        eng.wait_ge(self.c[0].h, self.c[0].n)
        eng.wait_ge(self.c[1].h, self.c[1].n)


def build_nc(phases=("T", "F1", "A", "F2"), dbg=()):
    nc = bass.Bass("TRN2", target_bir_lowering=False, detect_race_conditions=False)
    g = {"nc": nc, "phases": phases, "dbg_names": dbg}

    x_in = nc.dram_tensor("x", [L, D], F32, kind="ExternalInput").ap()
    out = nc.dram_tensor("out", [L, D], F32, kind="ExternalOutput").ap()
    g["x_in"], g["out"] = x_in, out
    g["wq_g"] = nc.dram_tensor("wq_gate", [HID, D], I32, kind="ExternalInput").ap()
    g["s_g"] = nc.dram_tensor("s_gate", [HID], F32, kind="ExternalInput").ap()
    g["wq_u"] = nc.dram_tensor("wq_up", [HID, D], I32, kind="ExternalInput").ap()
    g["s_u"] = nc.dram_tensor("s_up", [HID], F32, kind="ExternalInput").ap()
    g["wq_d"] = nc.dram_tensor("wq_down", [D, HID], I32, kind="ExternalInput").ap()
    g["s_d"] = nc.dram_tensor("s_down", [D], F32, kind="ExternalInput").ap()
    if "A" in phases:
        g["a_wmem"] = nc.dram_tensor("a_wmem", [D, ZC], F32, kind="ExternalInput").ap()
        g["a_kcol"] = nc.dram_tensor("a_kcol", [4, ZC], F32, kind="ExternalInput").ap()
        g["a_wq"] = nc.dram_tensor("a_wq", [D, QD], F32, kind="ExternalInput").ap()
        g["a_theta"] = nc.dram_tensor("a_theta", [128, MEM * M_], F32, kind="ExternalInput").ap()
        g["a_slopes"] = nc.dram_tensor("a_slopes", [K_, 1], F32, kind="ExternalInput").ap()
        g["a_sscale"] = nc.dram_tensor("a_sscale", [K_, 1], F32, kind="ExternalInput").ap()
        g["a_wblk"] = nc.dram_tensor("a_wblk", [2, 2, 128, 1536], F32, kind="ExternalInput").ap()
        g["a_norm"] = nc.dram_tensor("a_norm", [128, 2], F32, kind="ExternalInput").ap()
        g["a_wout"] = nc.dram_tensor("a_wout", [768, D], F32, kind="ExternalInput").ap()
        g["a_ramp"] = nc.dram_tensor("a_ramp", [K_, L], F32, kind="ExternalInput").ap()

    g["wgu_s"] = nc.dram_tensor("wgu_s", [NHB, 128, 2, NDB, 128], BF16, kind="Internal").ap()
    g["wd_s"] = nc.dram_tensor("wd_s", [NDB, 128, NHB, 128], BF16, kind="Internal").ap()
    g["xmid"] = nc.dram_tensor("xmid", [L, D], F32, kind="Internal").ap()
    g["cr_scr"] = nc.dram_tensor("cr_scr", [1, 4128], F32, kind="Internal").ap()
    g["dbg"] = {}
    _dbg_shapes = {"d_col": ([128, 8], F32), "d_h1": ([128, D], BF16),
                   "d_h1T": ([128, NDB * CH], BF16), "d_tmpT": ([128, 2 * CH], BF16),
                   "d_dn": ([128, NDB * CH], BF16), "d_wgu0": ([128, 2 * NDB * 128], BF16),
                   "d_wd0": ([128, NHB * 128], BF16)}
    for name in dbg:
        shp, dt = _dbg_shapes[name]
        g["dbg"][name] = nc.dram_tensor(name, shp, dt, kind="ExternalOutput").ap()

    g["ident_bf"] = nc.inline_tensor(np.eye(128, dtype=ml_dtypes.bfloat16), "ident_bf").ap()
    g["ident_f"] = nc.inline_tensor(np.eye(128, dtype=np.float32), "ident_f").ap()
    g["tri_bf_d"] = nc.inline_tensor(
        np.triu(np.ones((128, 128))).astype(ml_dtypes.bfloat16), "tri_bf").ap()
    g["ones_row_d"] = nc.inline_tensor(np.ones((1, 128), dtype=np.float32), "ones_row").ap()

    with contextlib.ExitStack() as ctx:
        cvals = sorted(set(float(v) for v in CONST_VALUES))
        cpool = ctx.enter_context(nc.sbuf_tensor("constpool", [128, len(cvals)], F32))
        for i, v in enumerate(cvals):
            nc.const_aps.register(F32, v, cpool[:, i:i + 1])
        g["cpool"], g["cvals"] = cpool, cvals

        sb_id = ctx.enter_context(nc.sbuf_tensor("sb_id", [128, 128], BF16))
        sb_idf = ctx.enter_context(nc.sbuf_tensor("sb_idf", [128, 128], F32))
        sb_tri = ctx.enter_context(nc.sbuf_tensor("sb_tri", [128, 128], BF16))
        sb_ones = ctx.enter_context(nc.sbuf_tensor("sb_ones", [1, 128], F32))
        sb_sg = ctx.enter_context(nc.sbuf_tensor("sb_sg", [128, NHB], F32))
        sb_su = ctx.enter_context(nc.sbuf_tensor("sb_su", [128, NHB], F32))
        sb_sd = ctx.enter_context(nc.sbuf_tensor("sb_sd", [128, NDB], F32))
        sb_sdh = ctx.enter_context(nc.sbuf_tensor("sb_sdh", [128, NDB], F32))
        g.update(sb_id=sb_id, sb_idf=sb_idf, sb_tri=sb_tri, sb_ones=sb_ones,
                 sb_sg=sb_sg, sb_su=sb_su, sb_sd=sb_sd, sb_sdh=sb_sdh)

        sem_names = ["c_ld", "t_dq", "t_tr", "t_ev", "t_ld_2", "a_sld", "vch", "a_sch"]
        if dbg:
            sem_names.append("dbg")
        dma_pairs = ["t_ld", "t_wb"]
        sem_names += [f"f_{s}" for s in (
            "sq", "ln1", "ln2", "h1", "htr", "hev",
            "mmg", "evg", "mmu", "evu", "mmd", "evd", "tr2", "xnew")]
        for p in ("f1", "f2"):
            dma_pairs += [f"{p}_{s}" for s in ("wld", "wdld", "x2", "xst")]
            sem_names += [f"{p}_xld_{k}" for k in range(4)]
        sem_names += [f"a_{s}" for s in (
            "sq", "ln1", "ln2", "h1", "htr", "hev", "mmz", "evz",
            "ztr", "zev", "cv", "pw", "pev", "ktr", "kev",
            "mg", "cs", "csev", "dot", "otr", "oev", "gim", "vrd",
            "ymm", "ysg", "ygt", "omm", "xnew")]
        dma_pairs += ["a_wld", "a_x2", "a_xst", "a_cr"]
        sem_names += [f"a_xld_{k}" for k in range(4)]
        for s in dma_pairs:
            sem_names += [s + "_0", s + "_1"]
        sems = {}
        for s in sem_names:
            sems[s] = Ctr(ctx.enter_context(nc.semaphore(s)))
        for s in dma_pairs:
            sems[s] = DmaPair(sems[s + "_0"], sems[s + "_1"])
        g["sems"] = sems

        with nc.Block() as block:
            @block.gpsimd
            def _(gp):
                c = sems["c_ld"]
                c.inc(gp.dma_start(sb_id[:], g["ident_bf"][:]), 16)
                c.inc(gp.dma_start(sb_idf[:], g["ident_f"][:]), 16)
                c.inc(gp.dma_start(sb_tri[:], g["tri_bf_d"][:]), 16)
                c.inc(gp.dma_start(sb_ones[:], g["ones_row_d"][:]), 16)
                c.inc(gp.dma_start(sb_sg[:], g["s_g"].rearrange("(a p) -> p a", p=128),
                                   allow_slow_non_contiguous=True), 16)
                c.inc(gp.dma_start(sb_su[:], g["s_u"].rearrange("(a p) -> p a", p=128),
                                   allow_slow_non_contiguous=True), 16)
                c.inc(gp.dma_start(sb_sd[:], g["s_d"].rearrange("(a p) -> p a", p=128),
                                   allow_slow_non_contiguous=True), 16)

            @block.vector
            def _(v):
                for i, val in enumerate(cvals):
                    v.memset(cpool[:, i:i + 1], val)
                v.wait_ge(sems["c_ld"].h, sems["c_ld"].n)
                v.tensor_scalar_mul(sb_sdh[:], sb_sd[:], 0.5)

        if "T" in phases:
            emit_transcode(nc, g)
        if "F1" in phases:
            dst = g["xmid"] if ("A" in phases or "F2" in phases) else out
            emit_ffn(nc, g, "f1", x_in, dst)
        if "A" in phases:
            src = g["xmid"] if "F1" in phases else x_in
            dst = g["xmid"] if "F2" in phases else out
            emit_attn(nc, g, src, dst)
        if "F2" in phases:
            src = g["xmid"] if ("F1" in phases or "A" in phases) else x_in
            emit_ffn(nc, g, "f2", src, out)

    return nc


def emit_transcode(nc, g):
    sems = g["sems"]
    t_wb = sems["t_wb"]
    t_ld3 = [sems["t_ld_0"], sems["t_ld_1"], sems["t_ld_2"]]
    t_dq, t_tr, t_ev = sems["t_dq"], sems["t_tr"], sems["t_ev"]
    NJ2 = 3 * (NHB // 2)
    with contextlib.ExitStack() as ctx:
        w_i32 = [ctx.enter_context(nc.sbuf_tensor(f"t_wi{i}", [128, 2 * D], I32)) for i in range(3)]
        w_bf = [ctx.enter_context(nc.sbuf_tensor(f"t_wb{i}", [128, 2 * D], BF16)) for i in range(2)]
        stage = [ctx.enter_context(nc.sbuf_tensor(f"t_st{i}", [128, 2 * D], BF16)) for i in range(2)]
        ps_tr = ctx.enter_context(nc.psum_tensor("t_ps", [128, 2, D], BF16))

        def job_info(job):
            mat, k = divmod(job, NHB // 2)
            return mat, 2 * k

        with nc.Block() as block:
            @block.gpsimd
            def _(gp):
                def load(job):
                    mat, hb = job_info(job)
                    buf = job % 3
                    if job >= 3:
                        gp.wait_ge(t_dq.h, job - 2)
                    if mat == 0:
                        srcl = g["wq_g"][hb * 128:(hb + 2) * 128, :] \
                            .rearrange("(h p) c -> p h c", p=128)
                    elif mat == 1:
                        srcl = g["wq_u"][hb * 128:(hb + 2) * 128, :] \
                            .rearrange("(h p) c -> p h c", p=128)
                    else:
                        srcl = g["wq_d"][:, hb * 128:(hb + 2) * 128] \
                            .rearrange("(a p) c -> p a c", p=128)
                    t_ld3[job % 3].inc(gp.dma_start(w_i32[buf][:], srcl), 16)
                def wb(job):
                    mat, hb = job_info(job)
                    buf = job % 2
                    gp.wait_ge(t_ev.h, 2 * (job + 1))
                    for half in range(2):
                        srcv = stage[buf][:, half * D:(half + 1) * D] \
                            .rearrange("p (a c) -> p a c", c=128)
                        if mat < 2:
                            dstv = g["wgu_s"][hb + half, :, mat, :, :]
                        else:
                            dstv = g["wd_s"][:, :, hb + half, :].transpose([1, 0, 2])
                        ins = gp.dma_start(dstv, srcv)
                        t_wb.c[job % 2].inc(ins, 16)
                load(0)
                load(1)
                load(2)
                for job in range(NJ2):
                    if job + 3 < NJ2:
                        load(job + 3)
                    wb(job)
                t_wb.wait_all(gp)

            @block.scalar
            def _(sc):
                for job in range(NJ2):
                    sc.wait_ge(t_ld3[job % 3].h, 16 * (job // 3 + 1))
                    buf3 = job % 3
                    if job >= 2:
                        sc.wait_ge(t_tr.h, 2 * (job - 1))
                    ins = sc.activation(out=w_bf[job % 2][:], in_=w_i32[buf3][:], func=AF.Identity,
                                        bias=-128.0, scale=1.0)
                    t_dq.inc(ins, 1)

            @block.tensor
            def _(t):
                for job in range(NJ2):
                    mat, hb = job_info(job)
                    t.wait_ge(t_dq.h, job + 1)
                    buf = job % 2
                    for half in range(2):
                        if 2 * job + half >= 2:
                            t.wait_ge(t_ev.h, 2 * job + half - 1)
                        for db in range(NDB):
                            if mat < 2:
                                src = w_bf[buf][:, half * D + db * 128:half * D + (db + 1) * 128]
                            else:
                                src = w_bf[buf][:, db * 256 + half * 128:db * 256 + (half + 1) * 128]
                            ins = t.transpose(out=ps_tr[:, half, db * 128:(db + 1) * 128],
                                              in_=src, identity=g["sb_id"][:])
                        t_tr.inc(ins, 1)

            @block.vector
            def _(v):
                for job in range(NJ2):
                    buf = job % 2
                    for half in range(2):
                        v.wait_ge(t_tr.h, 2 * job + half + 1)
                        if job >= 2:
                            jj = job - 2
                            v.wait_ge(t_wb.c[jj % 2].h, 32 * (jj // 2 + 1))
                        v.tensor_copy(stage[buf][:, half * D:half * D + 1024],
                                      ps_tr[:, half, 0:1024])
                        ins = v.tensor_copy(stage[buf][:, half * D + 1024:half * D + 2048],
                                            ps_tr[:, half, 1024:2048])
                        t_ev.inc(ins, 1)


def emit_ffn(nc, g, p, src_x, dst_x):
    sems = g["sems"]
    _shared = {"sq", "ln1", "ln2", "h1", "htr", "hev", "mmg", "evg", "mmu", "evu",
               "mmd", "evd", "tr2", "xnew"}
    S = lambda s: sems[f"f_{s}"] if s in _shared else sems[f"{p}_{s}"]
    PH = 0 if p == "f1" else 1
    J0 = PH * NTT
    I0 = PH * NCH * NHB
    D0 = PH * NCH * NDB
    xldq = [sems[f"{p}_xld_{k}"] for k in range(4)]

    def xld_wait(eng, j):
        eng.wait_ge(xldq[j % 4].h, 16 * (j // 4 + 1))
    with contextlib.ExitStack() as ctx:
        def sqt(j):
            return x_t[(j - 1) % 4] if j >= 1 else xn[0]
        x_t = [ctx.enter_context(nc.sbuf_tensor(f"{p}_x{i}", [128, D], F32)) for i in range(4)]
        col = ctx.enter_context(nc.sbuf_tensor(f"{p}_col", [128, 8], F32))
        h1 = [ctx.enter_context(nc.sbuf_tensor(f"{p}_h1{i}", [128, D], BF16)) for i in range(4)]
        h1T = [ctx.enter_context(nc.sbuf_tensor(f"{p}_h1T{i}", [128, NDB, CH], BF16)) for i in range(2)]
        wgu = [ctx.enter_context(nc.sbuf_tensor(f"{p}_wgu{i}", [128, 2, NDB, 128], BF16)) for i in range(2)]
        sgb = [ctx.enter_context(nc.sbuf_tensor(f"{p}_sg{i}", [128, CH], BF16)) for i in range(2)]
        tmpT = ctx.enter_context(nc.sbuf_tensor(f"{p}_tmpT", [128, NHB, CH], BF16))
        wd = [ctx.enter_context(nc.sbuf_tensor(f"{p}_wd{i}", [128, NHB, 128], BF16)) for i in range(2)]
        dn = ctx.enter_context(nc.sbuf_tensor(f"{p}_dn", [128, NDB, CH], BF16))
        xo = [ctx.enter_context(nc.sbuf_tensor(f"{p}_xo0", [128, D], F32))]
        xn = [ctx.enter_context(nc.sbuf_tensor(f"{p}_xn{i}", [128, D], F32)) for i in range(2)]
        ps_gu = ctx.enter_context(nc.psum_tensor(f"{p}_psgu", [128, 4, 512], F32))
        ps_tr = ctx.enter_context(nc.psum_tensor(f"{p}_pstr", [128, 2, D], BF16))

        with nc.Block() as block:
            @block.gpsimd
            def _(gp):
                dbg = g["dbg"]
                def xload1(j):
                    if j >= 4:
                        gp.wait_ge(S("ln2").h, 2 * (J0 + j - 3))
                        gp.wait_ge(S("ln1").h, J0 + j - 2)
                    r0 = j * 128
                    xldq[j % 4].inc(gp.dma_start(x_t[j % 4][:], src_x[r0:r0 + 128, :]), 16)
                def xloads(ch):
                    for tt in range(TCH):
                        xload1(ch * TCH + tt)
                def wguloads(ch):
                    if ch >= 1:
                        for tt in range(TCH):
                            xopair(ch - 1, tt)
                    for hb in range(NHB):
                        i = ch * NHB + hb
                        if i >= 2:
                            gp.wait_ge(S("mmu").h, I0 + i - 1)
                        S("wld").inc(i, gp.dma_start(wgu[i % 2][:], g["wgu_s"][hb, :, :, :, :]))
                        if hb % 11 == 10 and ch + 1 < NCH:
                            xload1((ch + 1) * TCH + hb // 11)
                def wdloads(ch):
                    for db in range(NDB):
                        i = ch * NDB + db
                        if i >= 2:
                            gp.wait_ge(S("mmd").h, D0 + i - 1)
                        S("wdld").inc(i, gp.dma_start(wd[i % 2][:], g["wd_s"][db, :, :, :]))
                def xopair(ch, tt):
                    j = ch * TCH + tt
                    if j >= 1:
                        gp.wait_ge(S("xnew").h, J0 + j)
                    r0 = j * 128
                    S("x2").inc(j, gp.dma_start(xo[0][:], src_x[r0:r0 + 128, :]))
                    gp.wait_ge(S("xnew").h, J0 + j + 1)
                    S("xst").inc(j, gp.dma_start(dst_x[r0:r0 + 128, :], xn[j % 2][:]))
                xloads(0)
                wguloads(0)
                wdloads(0)
                for ch in range(1, NCH):
                    wguloads(ch)
                    wdloads(ch)
                for tt in range(TCH):
                    xopair(NCH - 1, tt)
                S("xst").wait_all(gp)
            @block.vector
            def _(v):
                vch = sems["vch"]
                def VC(ins):
                    n = vch.inc(ins, 1)
                    v.wait_ge(vch.h, n)
                def ln_evac(j):
                    ch = j // TCH
                    v.wait_ge(S("htr").h, J0 + j + 1)
                    if ch >= 2:
                        v.wait_ge(S("mmu").h, I0 + (ch - 1) * NHB)
                    for half in range(2):
                        srcv = ps_tr[:, j % 2, half * 1024:(half + 1) * 1024] \
                            .rearrange("p (a c) -> p a c", c=128)
                        dstv = h1T[ch % 2][:, half * 8:(half + 1) * 8,
                                           (j % TCH) * 128:(j % TCH) * 128 + 128]
                        ins = v.tensor_copy(dstv, srcv)
                    S("hev").inc(ins, 1)
                def ln_tile(j):
                    b = j % 4
                    ch = j // TCH
                    xld_wait(v, j)
                    VC(v.tensor_reduce(col[:, 0:1], x_t[b][:], axis=AX.X, op=ALU.add))
                    VC(v.tensor_scalar_mul(col[:, 1:2], col[:, 0:1], 1.0 / D))
                    v.wait_ge(S("sq").h, J0 + j + 1)
                    VC(v.tensor_reduce(col[:, 2:3], sqt(j)[:], axis=AX.X, op=ALU.add))
                    VC(v.tensor_tensor(col[:, 3:4], col[:, 1:2], col[:, 1:2], op=ALU.mult))
                    VC(v.tensor_scalar_mul(col[:, 4:5], col[:, 2:3], 1.0 / D))
                    VC(v.tensor_tensor(col[:, 3:4], col[:, 4:5], col[:, 3:4], op=ALU.subtract))
                    ins = v.tensor_scalar_add(col[:, 3:4], col[:, 3:4], EPS_LN)
                    S("ln1").inc(ins, 1)
                    v.wait_ge(S("ln2").h, 2 * (J0 + j) + 1)
                    VC(v.reciprocal(col[:, 5:6], col[:, 4:5]))
                    ins = v.scalar_tensor_tensor(out=col[:, 6:7], in0=col[:, 1:2], scalar=-1.0,
                                                 in1=col[:, 5:6], op0=ALU.mult, op1=ALU.mult)
                    S("h1").inc(ins, 1)
                for tt in range(TCH):
                    ln_tile(tt)
                    ln_evac(tt)
                for ch in range(NCH):
                    for hb in range(NHB):
                        i = ch * NHB + hb
                        bank_u = (hb % 2) * 2 + 1
                        v.wait_ge(S("mmu").h, I0 + i + 1)
                        v.wait_ge(S("evg").h, I0 + i + 1)
                        ins = v.scalar_tensor_tensor(
                            out=tmpT[:, hb, :], in0=ps_gu[:, bank_u, :],
                            scalar=g["sb_su"][:, hb:hb + 1], in1=sgb[hb % 2][:],
                            op0=ALU.mult, op1=ALU.mult)
                        S("evu").inc(ins, 1)
                        if ch + 1 < NCH and hb % 11 == 10:
                            ln_tile((ch + 1) * TCH + hb // 11)
                    for db in range(NDB):
                        i = ch * NDB + db
                        v.wait_ge(S("mmd").h, D0 + i + 1)
                        ins = v.tensor_scalar_mul(dn[:, db, :], ps_gu[:, db % 4, :],
                                                  g["sb_sdh"][:, db:db + 1])
                        S("evd").inc(ins, 1)
                    for tt in range(TCH):
                        j = ch * TCH + tt
                        b = j % 2
                        v.wait_ge(S("tr2").h, J0 + j + 1)
                        S("x2").wait(v, j)
                        if j >= 2:
                            S("xst").wait(v, j - 2)
                        ins = v.tensor_tensor(out=xn[b][:], in0=ps_tr[:, b, :],
                                              in1=xo[0][:], op=ALU.add)
                        S("xnew").inc(ins, 1)
                    if ch + 1 < NCH:
                        for tt in range(TCH):
                            ln_evac((ch + 1) * TCH + tt)

            @block.scalar
            def _(sc):
                def ln_tile_s(j):
                    b4 = j % 4
                    xld_wait(sc, j)
                    if j >= 1:
                        sc.wait_ge(S("ln1").h, J0 + j)
                    ins = sc.activation(out=sqt(j)[:], in_=x_t[b4][:], func=AF.Square)
                    S("sq").inc(ins, 1)
                    sc.wait_ge(S("ln1").h, J0 + j + 1)
                    ins = sc.activation(out=col[:, 4:5], in_=col[:, 3:4], func=AF.Sqrt)
                    S("ln2").inc(ins, 1)
                    sc.wait_ge(S("h1").h, J0 + j + 1)
                    if j >= 4:
                        sc.wait_ge(S("htr").h, J0 + j - 3)
                    ins = sc.activation(out=h1[b4][:], in_=x_t[b4][:], func=AF.Identity,
                                        scale=col[:, 5:6], bias=col[:, 6:7])
                    S("ln2").inc(ins, 1)
                for tt in range(TCH):
                    ln_tile_s(tt)
                for ch in range(NCH):
                    for hb in range(NHB):
                        i = ch * NHB + hb
                        bank_g = (hb % 2) * 2
                        sc.wait_ge(S("mmg").h, I0 + i + 1)
                        if i >= 2:
                            sc.wait_ge(S("evu").h, I0 + i - 1)
                        ins = sc.activation(out=sgb[hb % 2][:], in_=ps_gu[:, bank_g, :],
                                            func=AF.Silu, scale=g["sb_sg"][:, hb:hb + 1])
                        S("evg").inc(ins, 1)
                        if ch + 1 < NCH and hb % 11 == 10:
                            ln_tile_s((ch + 1) * TCH + hb // 11)
            @block.tensor
            def _(t):
                for ch in range(NCH):
                    for tt in range(TCH):
                        j = ch * TCH + tt
                        b = j % 2
                        t.wait_ge(S("ln2").h, 2 * (J0 + j + 1))
                        if j >= 2:
                            t.wait_ge(S("hev").h, J0 + j - 1)
                        if ch >= 1:
                            t.wait_ge(S("xnew").h, J0 + (ch - 1) * TCH + TCH - 2 + (tt % 2) + 1)
                        for db in range(NDB):
                            ins = t.transpose(out=ps_tr[:, b, db * 128:(db + 1) * 128],
                                              in_=h1[j % 4][:, db * 128:(db + 1) * 128],
                                              identity=g["sb_id"][:])
                        S("htr").inc(ins, 1)
                    for hb in range(NHB):
                        i = ch * NHB + hb
                        bank_g = (hb % 2) * 2
                        bank_u = bank_g + 1
                        S("wld").wait(t, i)
                        if hb == 0:
                            t.wait_ge(S("hev").h, J0 + (ch + 1) * TCH)
                            t.wait_ge(S("evd").h, D0 + ch * NDB)
                        if i >= 2:
                            t.wait_ge(S("evg").h, I0 + i - 1)
                            t.wait_ge(S("evu").h, I0 + i - 1)
                        for db in range(NDB):
                            ins = t.matmul(ps_gu[:, bank_g, :], wgu[i % 2][:, 0, db, :],
                                           h1T[ch % 2][:, db, :], start=(db == 0),
                                           stop=(db == NDB - 1))
                        S("mmg").inc(ins, 1)
                        for db in range(NDB):
                            ins = t.matmul(ps_gu[:, bank_u, :], wgu[i % 2][:, 1, db, :],
                                           h1T[ch % 2][:, db, :], start=(db == 0),
                                           stop=(db == NDB - 1))
                        S("mmu").inc(ins, 1)
                    for db in range(NDB):
                        i = ch * NDB + db
                        S("wdld").wait(t, i)
                        if db == 0:
                            t.wait_ge(S("evu").h, I0 + (ch + 1) * NHB)
                            t.wait_ge(S("evg").h, I0 + (ch + 1) * NHB)
                        if i >= 4:
                            t.wait_ge(S("evd").h, D0 + i - 3)
                        for hb in range(NHB):
                            ins = t.matmul(ps_gu[:, db % 4, :], wd[i % 2][:, hb, :],
                                           tmpT[:, hb, :], start=(hb == 0), stop=(hb == NHB - 1))
                        S("mmd").inc(ins, 1)
                    for tt in range(TCH):
                        j = ch * TCH + tt
                        b = j % 2
                        t.wait_ge(S("evd").h, D0 + (ch + 1) * NDB)
                        if j >= 2:
                            t.wait_ge(S("xnew").h, J0 + j - 1)
                        for db in range(NDB):
                            ins = t.transpose(out=ps_tr[:, b, db * 128:(db + 1) * 128],
                                              in_=dn[:, db, tt * 128:(tt + 1) * 128],
                                              identity=g["sb_id"][:])
                        S("tr2").inc(ins, 1)


def emit_attn(nc, g, src_x, dst_x):
    sems = g["sems"]
    A = lambda s: sems[f"a_{s}"]
    axldq = [sems[f"a_xld_{k}"] for k in range(4)]

    def axld_wait(eng, j):
        eng.wait_ge(axldq[j % 4].h, 16 * (j // 4 + 1))
    NC9 = 9
    CW = [512] * 8 + [32]
    CO = [512 * i for i in range(9)]

    with contextlib.ExitStack() as actx:
        w_mem_bf = actx.enter_context(nc.sbuf_tensor("a_wmemb", [128, NDB, ZC], BF16))
        w_q_bf = actx.enter_context(nc.sbuf_tensor("a_wqb", [128, NDB, QD], BF16))
        wblk_bf = actx.enter_context(nc.sbuf_tensor("a_wblkb", [128, 2, 2, 1536], BF16))
        theta = actx.enter_context(nc.sbuf_tensor("a_theta_sb", [128, MEM * M_], F32))
        norm_sb = actx.enter_context(nc.sbuf_tensor("a_norm_sb", [128, 2], F32))
        acol = actx.enter_context(nc.sbuf_tensor("a_acol", [32, 8], F32))
        kc_sb = actx.enter_context(nc.sbuf_tensor("a_kc", [128, 12], F32))
        oT = [actx.enter_context(nc.sbuf_tensor(f"a_oT{i}", [128, 2, L], BF16)) for i in range(2)]

        with nc.Block() as block:
            @block.gpsimd
            def _(gp):
                c = A("wld")
                c.inc(0, gp.dma_start(
                    w_mem_bf[:], g["a_wmem"].rearrange("(a p) c -> p a c", p=128)))
                c.inc(1, gp.dma_start(
                    w_q_bf[:], g["a_wq"].rearrange("(a p) c -> p a c", p=128)))
                c.inc(2, gp.dma_start(theta[:], g["a_theta"][:]))
                c.inc(3, gp.dma_start(norm_sb[:], g["a_norm"][:]))
                sems["a_sld"].inc(gp.dma_start(acol[:, 0:1], g["a_slopes"][:]), 16)
                sems["a_sld"].inc(gp.dma_start(acol[:, 1:2], g["a_sscale"][:]), 16)
                for ct in range(3):
                    w = 128 if ct < 2 else 32
                    ins = gp.dma_start(kc_sb[0:w, ct * 4:(ct + 1) * 4],
                                       g["a_kcol"][:, ct * 128:ct * 128 + w].transpose([1, 0]),
                                       allow_slow_non_contiguous=True)
                    if ct < 2:
                        sems["a_sld"].inc(ins, 16)
                c.inc(6, ins)
                c.inc(7, gp.dma_start(wblk_bf[:], g["a_wblk"].transpose([2, 0, 1, 3])))
                gp.wait_ge(c.c[0].h, c.c[0].n)
                gp.wait_ge(c.c[1].h, c.c[1].n)
                gp.wait_ge(sems["a_sld"].h, sems["a_sld"].n)

            @block.vector
            def _(v):
                vch = sems["vch"]
                def VC(ins):
                    n = vch.inc(ins, 1)
                    v.wait_ge(vch.h, n)
                v.wait_ge(A("wld").c[0].h, A("wld").c[0].n)
                v.wait_ge(A("wld").c[1].h, A("wld").c[1].n)
                v.wait_ge(sems["a_sld"].h, sems["a_sld"].n)
                for ri in range(2):
                    for kt in range(2):
                        VC(v.tensor_scalar_mul(wblk_bf[:, ri, kt, :], wblk_bf[:, ri, kt, :],
                                               norm_sb[:, kt:kt + 1]))
                v.wait_ge(A("sq").h, 1)
                VC(v.tensor_scalar_max(acol[0:32, 5:6], acol[0:32, 0:1], 0.0))
                VC(v.tensor_add(acol[0:32, 6:7], acol[0:32, 5:6], acol[0:32, 4:5]))
                ins = v.tensor_scalar_mul(acol[0:32, 7:8], acol[0:32, 6:7], -1.0)
                A("cv").inc(ins, 1)

            @block.scalar
            def _(sc):
                sch = sems["a_sch"]
                def SC(ins):
                    n = sch.inc(ins, 1)
                    sc.wait_ge(sch.h, n)
                sc.wait_ge(sems["a_sld"].h, 64)
                SC(sc.activation(out=acol[0:32, 2:3], in_=acol[0:32, 0:1], func=AF.Abs))
                SC(sc.activation(out=acol[0:32, 3:4], in_=acol[0:32, 2:3], func=AF.Exp,
                                 scale=-1.0))
                ins = sc.activation(out=acol[0:32, 4:5], in_=acol[0:32, 3:4], func=AF.Ln,
                                    bias=1.0)
                A("sq").inc(ins, 1)

        with contextlib.ExitStack() as zctx:
            k_val = zctx.enter_context(nc.sbuf_tensor("a_kval", [128, NTT, MEM], F32))
            p_w = zctx.enter_context(nc.sbuf_tensor("a_pw", [128, NTT, K_], F32))
            q_sb = zctx.enter_context(nc.sbuf_tensor("a_qsb", [128, NTT, QD], BF16))
            zscope = contextlib.ExitStack()
            z_sb = zscope.enter_context(nc.sbuf_tensor("a_zsb", [128, NTT, ZC], F32))

            with contextlib.ExitStack() as ctx:
                x_t = [ctx.enter_context(nc.sbuf_tensor(f"a_x{i}", [128, D], F32)) for i in range(4)]
                sq_dump = [ctx.enter_context(nc.sbuf_tensor(f"a_sqd{i}", [128, D], F32)) for i in range(2)]
                col = ctx.enter_context(nc.sbuf_tensor("a_col", [128, 2, 8], F32))
                h1 = [ctx.enter_context(nc.sbuf_tensor(f"a_h1{i}", [128, D], BF16)) for i in range(4)]
                h2T = ctx.enter_context(nc.sbuf_tensor("a_h2T", [128, NDB, 128], BF16))
                ps_tr = ctx.enter_context(nc.psum_tensor("a_pstr", [128, 2, D], BF16))
                ps_z = ctx.enter_context(nc.psum_tensor("a_psz", [128, 2, 512], F32))
                ps_q = ctx.enter_context(nc.psum_tensor("a_psq", [128, 2, 512], F32))

                with nc.Block() as block:
                    @block.gpsimd
                    def _(gp):
                        for j in range(NTT):
                            if j >= 4:
                                gp.wait_ge(A("ln2").h, 2 * (j - 3))
                                gp.wait_ge(A("ln1").h, j - 2)
                            axldq[j % 4].inc(gp.dma_start(
                                x_t[j % 4][:], src_x[j * 128:(j + 1) * 128, :]), 16)
                        gp.wait_ge(A("evz").h, NTT)

                    @block.vector
                    def _(v):
                        vch = sems["vch"]
                        def VC(ins):
                            n = vch.inc(ins, 1)
                            v.wait_ge(vch.h, n)
                        for j in range(NTT):
                            b = j % 2
                            cl = col[:, j % 2, :]
                            axld_wait(v, j)
                            if j >= 2:
                                v.wait_ge(A("ln2").h, 2 * (j - 1))
                            VC(v.tensor_reduce(cl[:, 0:1], x_t[j % 4][:], axis=AX.X, op=ALU.add))
                            VC(v.tensor_scalar_mul(cl[:, 1:2], cl[:, 0:1], 1.0 / D))
                            v.wait_ge(A("sq").h, 1 + j + 1)
                            VC(v.tensor_reduce(cl[:, 2:3], sq_dump[j % 2][:], axis=AX.X, op=ALU.add))
                            VC(v.tensor_tensor(cl[:, 3:4], cl[:, 1:2], cl[:, 1:2], op=ALU.mult))
                            VC(v.tensor_scalar_mul(cl[:, 4:5], cl[:, 2:3], 1.0 / D))
                            VC(v.tensor_tensor(cl[:, 3:4], cl[:, 4:5], cl[:, 3:4], op=ALU.subtract))
                            ins = v.tensor_scalar_add(cl[:, 3:4], cl[:, 3:4], EPS_LN)
                            A("ln1").inc(ins, 1)
                            v.wait_ge(A("ln2").h, 2 * j + 1)
                            VC(v.reciprocal(cl[:, 5:6], cl[:, 4:5]))
                            ins = v.scalar_tensor_tensor(out=cl[:, 6:7], in0=cl[:, 1:2], scalar=-1.0,
                                                         in1=cl[:, 5:6], op0=ALU.mult, op1=ALU.mult)
                            A("h1").inc(ins, 1)
                            v.wait_ge(A("htr").h, j + 1)
                            for half in range(2):
                                srcv = ps_tr[:, b, half * 1024:(half + 1) * 1024] \
                                    .rearrange("p (a c) -> p a c", c=128)
                                dstv = h2T[:, half * 8:(half + 1) * 8, :]
                                ins = v.tensor_copy(dstv, srcv)
                            A("hev").inc(ins, 1)
                            v.wait_ge(A("mmz").h, 2 * (j + 1))
                            v.tensor_copy(z_sb[:, j, :], ps_z[:, b, 0:ZC])
                            ins = v.tensor_copy(q_sb[:, j, :], ps_q[:, b, :])
                            A("evz").inc(ins, 1)

                    @block.scalar
                    def _(sc):
                        def square(j):
                            axld_wait(sc, j)
                            if j >= 2:
                                sc.wait_ge(A("ln1").h, j - 1)
                            ins = sc.activation(out=sq_dump[j % 2][:], in_=x_t[j % 4][:],
                                                func=AF.Square)
                            A("sq").inc(ins, 1)
                        square(0)
                        for j in range(NTT):
                            cl = col[:, j % 2, :]
                            sc.wait_ge(A("ln1").h, j + 1)
                            ins = sc.activation(out=cl[:, 4:5], in_=cl[:, 3:4], func=AF.Sqrt)
                            A("ln2").inc(ins, 1)
                            if j + 1 < NTT:
                                square(j + 1)
                            sc.wait_ge(A("h1").h, j + 1)
                            if j >= 4:
                                sc.wait_ge(A("htr").h, j - 3)
                            ins = sc.activation(out=h1[j % 4][:], in_=x_t[j % 4][:], func=AF.Identity,
                                                scale=cl[:, 5:6], bias=cl[:, 6:7])
                            A("ln2").inc(ins, 1)

                    @block.tensor
                    def _(t):
                        for j in range(NTT):
                            b = j % 2
                            t.wait_ge(A("ln2").h, 2 * (j + 1))
                            if j >= 2:
                                t.wait_ge(A("hev").h, j - 1)
                            for db in range(NDB):
                                ins = t.transpose(out=ps_tr[:, b, db * 128:(db + 1) * 128],
                                                  in_=h1[j % 4][:, db * 128:(db + 1) * 128],
                                                  identity=g["sb_id"][:])
                            A("htr").inc(ins, 1)
                            t.wait_ge(A("hev").h, j + 1)
                            if j >= 2:
                                t.wait_ge(A("evz").h, j - 1)
                            for db in range(NDB):
                                ins = t.matmul(ps_z[:, b, 0:ZC], h2T[:, db, :],
                                               w_mem_bf[:, db, :], start=(db == 0),
                                               stop=(db == NDB - 1))
                            A("mmz").inc(ins, 1)
                            for db in range(NDB):
                                ins = t.matmul(ps_q[:, b, :], h2T[:, db, :],
                                               w_q_bf[:, db, :], start=(db == 0),
                                               stop=(db == NDB - 1))
                            A("mmz").inc(ins, 1)

            with contextlib.ExitStack() as ctx:
                zT = ctx.enter_context(nc.sbuf_tensor("a_zT", [128, 3, 2052], F32))
                zc = ctx.enter_context(nc.sbuf_tensor("a_zc", [128, 3, L], F32))
                ramp = ctx.enter_context(nc.sbuf_tensor("a_ramp_sb", [32, L], F32))
                pwT = ctx.enter_context(nc.sbuf_tensor("a_pwT", [32, L], F32))
                ps_f = ctx.enter_context(nc.psum_tensor("a_psf", [128, 2, D], F32))

                with nc.Block() as block:
                    @block.gpsimd
                    def _(gp):
                        nr = sems["a_sld"].inc(gp.dma_start(ramp[:], g["a_ramp"][:]), 16)
                        gp.wait_ge(A("kev").h, 2 * NTT)

                    @block.vector
                    def _(v):
                        vch = sems["vch"]
                        def VC(ins):
                            n = vch.inc(ins, 1)
                            v.wait_ge(vch.h, n)
                        for ct in range(3):
                            v.memset(zT[:, ct, 0:4], 0.0)
                        ins = v.memset(pwT[0:1, 0:1], 0.0)
                        A("ztr").inc(ins, 1)
                        for ct in range(3):
                            v.wait_ge(A("ztr").h, ct + 2)
                            w = 128 if ct < 2 else 32
                            ins = v.tensor_copy(zT[0:w, ct, 4:2052], ps_f[0:w, ct % 2, :])
                            A("zev").inc(ins, 1)
                        for ct in range(3):
                            w = 128 if ct < 2 else 32
                            v.wait_ge(A("zev").h, ct + 1)
                            v.tensor_scalar_mul(zc[0:w, ct, :], zT[0:w, ct, 4:2052],
                                                kc_sb[0:w, ct * 4 + 3:ct * 4 + 4])
                            for jj in (2, 1, 0):
                                v.scalar_tensor_tensor(
                                    out=zc[0:w, ct, :], in0=zT[0:w, ct, jj + 1:jj + 1 + 2048],
                                    scalar=kc_sb[0:w, ct * 4 + jj:ct * 4 + jj + 1],
                                    in1=zc[0:w, ct, :], op0=ALU.mult, op1=ALU.add)
                            ins = v.tensor_copy(zc[0:1, ct, 0:1], zc[0:1, ct, 0:1])
                            A("cv").inc(ins, 1)
                        v.wait_ge(A("cv").h, 4)
                        v.wait_ge(sems["a_sld"].h, sems["a_sld"].n)
                        VC(v.tensor_scalar(pwT[:], zc[0:32, 2, :], acol[0:32, 1:2], 20.0,
                                           op0=ALU.mult, op1=ALU.min))
                        VC(v.tensor_scalar_max(pwT[:], pwT[:], -20.0))
                        VC(v.tensor_scalar_mul(ramp[:], ramp[:], acol[0:32, 7:8]))
                        ins = v.tensor_tensor(pwT[:], pwT[:], ramp[:], op=ALU.add)
                        A("pw").inc(ins, 1)
                        for j in range(NTT):
                            v.wait_ge(A("ktr").h, j + 1)
                            v.tensor_copy(k_val[:, j, :], ps_f[:, j % 2, 0:MEM])
                            ins = v.tensor_copy(p_w[:, j, :], ps_f[:, j % 2, MEM:MEM + K_])
                            A("kev").inc(ins, 2)

                    @block.scalar
                    def _(sc):
                        sc.wait_ge(A("pw").h, 1)
                        ins = sc.activation(out=pwT[:], in_=pwT[:], func=AF.Exp)
                        A("pw").inc(ins, 1)

                    @block.tensor
                    def _(t):
                        for ct in range(3):
                            w = 128 if ct < 2 else 32
                            if ct >= 2:
                                t.wait_ge(A("zev").h, ct - 1)
                            for j in range(NTT):
                                ins = t.transpose(
                                    out=ps_f[0:w, ct % 2, j * 128:(j + 1) * 128],
                                    in_=z_sb[:, j, ct * 128:ct * 128 + w],
                                    identity=g["sb_idf"][:])
                            A("ztr").inc(ins, 1)
                        t.wait_ge(A("pw").h, 2)
                        t.wait_ge(A("cv").h, 4)
                        for j in range(NTT):
                            if j >= 2:
                                t.wait_ge(A("kev").h, 2 * (j - 1))
                            for ct in range(2):
                                t.transpose(out=ps_f[:, j % 2, ct * 128:(ct + 1) * 128],
                                            in_=zc[:, ct, j * 128:(j + 1) * 128],
                                            identity=g["sb_idf"][:])
                            ins = t.transpose(out=ps_f[:, j % 2, MEM:MEM + K_],
                                              in_=pwT[0:32, j * 128:(j + 1) * 128],
                                              identity=g["sb_idf"][0:32, 0:32])
                            A("ktr").inc(ins, 1)

            zscope.close()
            with contextlib.ExitStack() as ctx:
                phi = ctx.enter_context(nc.sbuf_tensor("a_phi", [128, 2048], F32))
                trg = ctx.enter_context(nc.sbuf_tensor("a_trg", [128, 2048], F32))
                prod = ctx.enter_context(nc.sbuf_tensor("a_prod", [128, 2048], F32))
                prod2 = ctx.enter_context(nc.sbuf_tensor("a_prod2", [128, 2048], F32))
                merged = [ctx.enter_context(nc.sbuf_tensor(f"a_mg{i}", [128, 4128], BF16))
                          for i in range(2)]
                cs = [ctx.enter_context(nc.sbuf_tensor(f"a_cs{i}", [128, 4128], F32))
                      for i in range(2)]
                prod3 = ctx.enter_context(nc.sbuf_tensor("a_prod3", [128, 2048], BF16))
                prod4 = ctx.enter_context(nc.sbuf_tensor("a_prod4", [128, 2048], BF16))
                ore = ctx.enter_context(nc.sbuf_tensor("a_ore", [128, MEM], F32))
                oim = ctx.enter_context(nc.sbuf_tensor("a_oim", [128, MEM], F32))
                ot1 = ctx.enter_context(nc.sbuf_tensor("a_ot1", [128, 2 * K_], F32))
                obf = ctx.enter_context(nc.sbuf_tensor("a_obf", [128, 2, MEM], BF16))
                carry = ctx.enter_context(nc.sbuf_tensor("a_carry", [128, 4128], F32))
                ps_cs = ctx.enter_context(nc.psum_tensor("a_pscs", [128, 2, 512], F32))
                ps_o = ctx.enter_context(nc.psum_tensor("a_pso", [128, 2, MEM], BF16))

                def qviews(b, j, pa, pb):
                    vs = []
                    for kq in range(KQ):
                        base = kq * 128
                        qre_v = q_sb[:, j, base:base + 127:2].rearrange("p (h m) -> p h m", m=M_) \
                            .unsqueeze(1).broadcast_to((128, 8, 8, M_))
                        qim_v = q_sb[:, j, base + 1:base + 128:2].rearrange("p (h m) -> p h m", m=M_) \
                            .unsqueeze(1).broadcast_to((128, 8, 8, M_))
                        csr = cs[b][:, K_ + kq * 512:K_ + (kq + 1) * 512] \
                            .rearrange("p (r h m) -> p r h m", h=8, m=M_)
                        csi = cs[b][:, K_ + 2048 + kq * 512:K_ + 2048 + (kq + 1) * 512] \
                            .rearrange("p (r h m) -> p r h m", h=8, m=M_)
                        pr = pa[:, kq * 512:(kq + 1) * 512].rearrange(
                            "p (r h m) -> p r h m", h=8, m=M_)
                        pr2 = pb[:, kq * 512:(kq + 1) * 512].rearrange(
                            "p (r h m) -> p r h m", h=8, m=M_)
                        vs.append((qre_v, qim_v, csr, csi, pr, pr2))
                    return vs

                with nc.Block() as block:
                    @block.gpsimd
                    def _(gp):
                        cr0, cr1 = A("cr").c
                        for j in range(NTT):
                            gp.wait_ge(A("csev").h, NC9 * (j + 1))
                            if j < NTT - 1:
                                if j >= 1:
                                    gp.wait_ge(cr1.h, 16 * j)
                                cr0.inc(gp.dma_start(g["cr_scr"], cs[j % 2][127:128, :]), 16)
                                gp.wait_ge(cr0.h, 16 * (j + 1))
                                cr1.inc(gp.dma_start(
                                    carry[:], g["cr_scr"][0:1, :].broadcast_to((128, 4128))), 16)
                            if j >= 1:
                                gp.wait_ge(A("vrd").h, j)
                            for (qre_v, qim_v, csr, csi, pr, pr2) in qviews(
                                    j % 2, j, prod3, prod4):
                                gp.tensor_tensor(out=pr, in0=csi, in1=qre_v, op=ALU.mult)
                                ins = gp.tensor_tensor(out=pr2, in0=csr, in1=qim_v, op=ALU.mult)
                            A("gim").inc(ins, 1)
                        gp.wait_ge(A("oev").h, 2 * NTT)

                    @block.vector
                    def _(v):
                        vch = sems["vch"]
                        def VC(ins):
                            n = vch.inc(ins, 1)
                            v.wait_ge(vch.h, n)
                        def trig_pre(j):
                            kv = k_val[:, j, :].unsqueeze(2).broadcast_to((128, MEM, M_))
                            th3 = theta[:].rearrange("p (a c) -> p a c", c=M_)
                            ph3 = phi[:].rearrange("p (a c) -> p a c", c=M_)
                            v.tensor_tensor(out=ph3, in0=kv, in1=th3, op=ALU.mult)
                            v.tensor_scalar(prod[:], phi[:], INV2PI, MAGIC,
                                            op0=ALU.mult, op1=ALU.add)
                            v.tensor_scalar_sub(prod[:], prod[:], MAGIC)
                            v.scalar_tensor_tensor(out=trg[:], in0=prod[:], scalar=TWOPI,
                                                   in1=phi[:], op0=ALU.mult, op1=ALU.subtract)
                            ins = v.tensor_scalar(trg[:], trg[:], -PI_CLAMP, PI_CLAMP,
                                                  op0=ALU.max, op1=ALU.min)
                            A("mg").inc(ins, 1)
                        def merged_build(j):
                            b = j % 2
                            v.wait_ge(A("pev").h, 2 * j + 2)
                            if j >= 2:
                                v.wait_ge(A("cs").h, NC9 * (j - 1))
                            pw3 = p_w[:, j, :].unsqueeze(2).broadcast_to((128, K_, 64))
                            v.tensor_copy(merged[b][:, 0:K_], p_w[:, j, :])
                            mg3 = merged[b][:, K_:K_ + 2048].rearrange("p (a c) -> p a c", c=64)
                            v.tensor_tensor(out=mg3, in0=phi[:].rearrange("p (a c) -> p a c", c=64),
                                            in1=pw3, op=ALU.mult)
                            mg4 = merged[b][:, K_ + 2048:].rearrange("p (a c) -> p a c", c=64)
                            ins = v.tensor_tensor(out=mg4,
                                                  in0=trg[:].rearrange("p (a c) -> p a c", c=64),
                                                  in1=pw3, op=ALU.mult)
                            A("mg").inc(ins, 1)
                        trig_pre(0)
                        merged_build(0)
                        for j in range(NTT):
                            b = j % 2
                            for c9 in range(NC9):
                                v.wait_ge(A("cs").h, NC9 * j + c9 + 1)
                                if c9 == 0 and j >= 2:
                                    v.wait_ge(A("gim").h, j - 1)
                                if j == 0:
                                    ins = v.tensor_copy(cs[b][:, CO[c9]:CO[c9] + CW[c9]],
                                                        ps_cs[:, c9 % 2, 0:CW[c9]])
                                else:
                                    if c9 == 0:
                                        v.wait_ge(A("cr").c[1].h, 16 * j)
                                    ins = v.tensor_tensor(
                                        out=cs[b][:, CO[c9]:CO[c9] + CW[c9]],
                                        in0=ps_cs[:, c9 % 2, 0:CW[c9]],
                                        in1=carry[:, CO[c9]:CO[c9] + CW[c9]], op=ALU.add)
                                A("csev").inc(ins, 1)
                            if j + 1 < NTT:
                                trig_pre(j + 1)
                                merged_build(j + 1)
                            vs = qviews(b, j, prod, prod2)
                            for (qre_v, qim_v, csr, csi, pr, pr2) in vs:
                                v.tensor_tensor(out=pr, in0=csr, in1=qre_v, op=ALU.mult)
                                v.tensor_tensor(out=pr2, in0=csi, in1=qim_v, op=ALU.mult)
                            VC(v.tensor_add(prod[:], prod[:], prod2[:]))
                            for kq in range(KQ):
                                pr3 = prod[:, kq * 512:(kq + 1) * 512].rearrange(
                                    "p (a m) -> p a m", m=M_)
                                VC(v.tensor_reduce(ore[:, kq * 64:(kq + 1) * 64], pr3,
                                                   axis=AX.X, op=ALU.add))
                            v.wait_ge(A("gim").h, j + 1)
                            ins = v.tensor_tensor(out=prod[:], in0=prod3[:], in1=prod4[:],
                                                  op=ALU.subtract)
                            A("vrd").inc(ins, 1)
                            v.wait_ge(A("vrd").h, j + 1)
                            for kq in range(KQ):
                                pr3 = prod[:, kq * 512:(kq + 1) * 512].rearrange(
                                    "p (a m) -> p a m", m=M_)
                                VC(v.tensor_reduce(oim[:, kq * 64:(kq + 1) * 64], pr3,
                                                   axis=AX.X, op=ALU.add))
                            VC(v.tensor_scalar_max(ot1[:, 0:K_], cs[b][:, 0:K_], 1e-4))
                            VC(v.reciprocal(ot1[:, K_:2 * K_], ot1[:, 0:K_]))
                            idv = ot1[:, K_:2 * K_].unsqueeze(2).broadcast_to((128, K_, H_))
                            for ri, srcb in ((0, ore), (1, oim)):
                                VC(v.tensor_tensor(
                                    out=obf[:, ri, :].rearrange("p (a c) -> p a c", c=H_),
                                    in0=srcb[:].rearrange("p (a c) -> p a c", c=H_),
                                    in1=idv, op=ALU.mult))
                            A("dot").inc(v.tensor_copy(ot1[0:1, 0:1], ot1[0:1, 0:1]), 1)
                            v.wait_ge(A("otr").h, j + 1)
                            for ri in range(2):
                                srcv = ps_o[:, ri, :].rearrange("p (a c) -> p a c", c=128)
                                dstv = oT[ri][:, :, j * 128:(j + 1) * 128]
                                ins = v.tensor_copy(dstv, srcv)
                            A("oev").inc(ins, 2)

                    @block.scalar
                    def _(sc):
                        sch = sems["a_sch"]
                        def SC(ins):
                            n = sch.inc(ins, 1)
                            sc.wait_ge(sch.h, n)
                        for j in range(NTT):
                            sc.wait_ge(A("mg").h, 2 * j + 1)
                            if j >= 1:
                                sc.wait_ge(A("mg").h, 2 * j)
                            SC(sc.activation(out=phi[:], in_=trg[:], func=AF.Abs))
                            A("pev").inc(sc.activation(out=phi[:], in_=phi[:], func=AF.Sin,
                                                       bias=PI / 2, scale=-1.0), 1)
                            ins = sc.activation(out=trg[:], in_=trg[:], func=AF.Sin, scale=-1.0)
                            A("pev").inc(ins, 1)

                    @block.tensor
                    def _(t):
                        def otr(j):
                            t.wait_ge(A("dot").h, j + 1)
                            if j >= 2:
                                t.wait_ge(A("oev").h, 2 * (j - 1))
                            for ri in range(2):
                                for kt in range(2):
                                    ins = t.transpose(out=ps_o[:, ri, kt * 128:(kt + 1) * 128],
                                                      in_=obf[:, ri, kt * 128:(kt + 1) * 128],
                                                      identity=g["sb_id"][:])
                            A("otr").inc(ins, 1)
                        for j in range(NTT):
                            b = j % 2
                            if j >= 1:
                                otr(j - 1)
                            t.wait_ge(A("mg").h, 2 * (j + 1))
                            for c9 in range(NC9):
                                if NC9 * j + c9 >= 2:
                                    t.wait_ge(A("csev").h, NC9 * j + c9 - 1)
                                ins = t.matmul(ps_cs[:, c9 % 2, 0:CW[c9]], g["sb_tri"][:],
                                               merged[b][:, CO[c9]:CO[c9] + CW[c9]],
                                               start=True, stop=True)
                                A("cs").inc(ins, 1)
                        otr(NTT - 1)

        with contextlib.ExitStack() as ctx:
            w_out_bf = ctx.enter_context(nc.sbuf_tensor("a_woutb", [128, 6, D], BF16))
            sg_sb = ctx.enter_context(nc.sbuf_tensor("a_sg", [128, 6, 512], BF16))
            gT = ctx.enter_context(nc.sbuf_tensor("a_gT", [128, 6, 512], BF16))
            xo = [ctx.enter_context(nc.sbuf_tensor(f"a_xo{i}", [128, D], F32)) for i in range(2)]
            xn = [ctx.enter_context(nc.sbuf_tensor(f"a_xn{i}", [128, D], F32)) for i in range(2)]
            ps_y = ctx.enter_context(nc.psum_tensor("a_psy", [128, 2, 512], F32))
            ps_x = ctx.enter_context(nc.psum_tensor("a_psx", [128, D], F32))

            yorder = list(range(6, 12)) + list(range(0, 6))
            ycons = []

            wout_n = [0]
            with nc.Block() as block:
                @block.gpsimd
                def _(gp):
                    wout_n[0] = A("wld").c[0].inc(gp.dma_start(
                        w_out_bf[:], g["a_wout"].rearrange("(a p) c -> p a c", p=128)), 16)
                    for j in range(NTT):
                        if j >= 2:
                            gp.wait_ge(A("xnew").h, j - 1)
                        A("x2").inc(j, gp.dma_start(xo[j % 2][:], src_x[j * 128:(j + 1) * 128, :]))
                        gp.wait_ge(A("xnew").h, j + 1)
                        A("xst").inc(j, gp.dma_start(dst_x[j * 128:(j + 1) * 128, :], xn[j % 2][:]))
                    A("xst").wait_all(gp)

                @block.vector
                def _(v):
                    for cq in range(4):
                        for mi, mb in enumerate(range(0, 6)):
                            i = cq * 12 + 6 + mi
                            v.wait_ge(A("ymm").h, i + 1)
                            v.wait_ge(A("ysg").h, cq * 6 + mb + 1)
                            if cq >= 1:
                                v.wait_ge(A("omm").h, 4 * cq)
                            ins = v.tensor_tensor(out=gT[:, mb, :], in0=ps_y[:, i % 2, :],
                                                  in1=sg_sb[:, mb, :], op=ALU.mult)
                            A("ygt").inc(ins, 1)
                        for tt in range(4):
                            j = cq * 4 + tt
                            bb = j % 2
                            v.wait_ge(A("omm").h, j + 1)
                            A("x2").wait(v, j)
                            if j >= 2:
                                A("xst").wait(v, j - 2)
                            ins = v.tensor_tensor(out=xn[bb][:], in0=ps_x[:], in1=xo[bb][:],
                                                  op=ALU.add)
                            A("xnew").inc(ins, 1)

                @block.scalar
                def _(sc):
                    for cq in range(4):
                        for gi in range(6):
                            i = cq * 12 + gi
                            sc.wait_ge(A("ymm").h, i + 1)
                            if cq >= 1:
                                sc.wait_ge(A("ygt").h, (cq - 1) * 6 + gi + 1)
                            ins = sc.activation(out=sg_sb[:, gi, :], in_=ps_y[:, i % 2, :],
                                                func=AF.Silu)
                            A("ysg").inc(ins, 1)

                @block.tensor
                def _(t):
                    for cq in range(4):
                        for idx, mb in enumerate(yorder):
                            i = cq * 12 + idx
                            if mb < 6:
                                kt = 1 if mb >= 3 else 0
                                col0 = mb * 128
                            else:
                                kt = 1 if mb >= 9 else 0
                                col0 = 768 + (mb - 6) * 128
                            if i >= 2:
                                s_, n_ = ycons[i - 2]
                                t.wait_ge(s_.h, n_)
                            for ri in range(2):
                                ins = t.matmul(ps_y[:, i % 2, :],
                                               wblk_bf[:, ri, kt, col0:col0 + 128],
                                               oT[ri][:, kt, cq * 512:(cq + 1) * 512],
                                               start=(ri == 0), stop=(ri == 1))
                            A("ymm").inc(ins, 1)
                            if mb >= 6:
                                ycons.append((A("ysg"), cq * 6 + (mb - 6) + 1))
                            else:
                                ycons.append((A("ygt"), cq * 6 + mb + 1))
                        for tt in range(4):
                            j = cq * 4 + tt
                            if j == 0:
                                t.wait_ge(A("wld").c[0].h, wout_n[0])
                            t.wait_ge(A("ygt").h, cq * 6 + 6)
                            if j >= 1:
                                t.wait_ge(A("xnew").h, j)
                            for cb in range(6):
                                for nn_ in range(4):
                                    ins = t.matmul(ps_x[:, nn_ * 512:(nn_ + 1) * 512],
                                                   gT[:, cb, tt * 128:(tt + 1) * 128],
                                                   w_out_bf[:, cb, nn_ * 512:(nn_ + 1) * 512],
                                                   start=(cb == 0), stop=(cb == 5))
                            A("omm").inc(ins, 1)


def _host_layouts(inputs):
    wre = np.asarray(inputs["L0_attn_W_re"])
    wim = np.asarray(inputs["L0_attn_W_im"])
    blk = np.zeros((2, 2, 128, 1536), dtype=np.float32)
    for idx, w in enumerate((wre, wim)):
        for k in range(K_):
            kt, khl = divmod(k * H_, 128)
            rows = slice(khl, khl + H_)
            blk[idx, kt, rows, k * 24:(k + 1) * 24] = w[k, :, 0:24]
            blk[idx, kt, rows, 768 + k * 24:768 + (k + 1) * 24] = w[k, :, 24:48]
    theta = np.asarray(inputs["L0_attn_theta"]).reshape(1, MEM * M_)
    norm = np.asarray(inputs["L0_attn_norm_scale"]).reshape(1, MEM)
    slopes = np.concatenate([np.asarray(inputs["L0_attn_decay_slopes"]),
                             np.asarray(inputs["L0_attn_anchor_slopes"])]).reshape(K_, 1).astype(np.float32)
    t_ramp = np.arange(L, dtype=np.float32)
    ramp = np.zeros((K_, L), dtype=np.float32)
    ramp[0:28, :] = (L - 1) - t_ramp[None, :]
    ramp[28:32, :] = t_ramp[None, :]
    return {
        "a_wmem": np.ascontiguousarray(inputs["L0_attn_W_mem"], dtype=np.float32),
        "a_kcol": np.ascontiguousarray(np.asarray(inputs["L0_attn_conv_k"]).reshape(4, ZC)),
        "a_wq": np.ascontiguousarray(inputs["L0_attn_W_q"], dtype=np.float32),
        "a_theta": np.ascontiguousarray(np.broadcast_to(theta, (128, MEM * M_))),
        "a_slopes": slopes,
        "a_sscale": np.ascontiguousarray(
            np.asarray(inputs["L0_attn_score_scale"]).reshape(K_, 1), dtype=np.float32),
        "a_wblk": blk,
        "a_norm": np.ascontiguousarray(norm.reshape(2, 128).T),
        "a_wout": np.ascontiguousarray(inputs["L0_attn_W_out"], dtype=np.float32),
        "a_ramp": ramp,
    }


def kernel(**inputs):
    x = np.ascontiguousarray(np.asarray(inputs["L0_attn_x"], dtype=np.float32))
    B = x.shape[0]
    base = {
        "wq_gate": np.ascontiguousarray(inputs["L0_ffn_wq_gate"]),
        "s_gate": np.ascontiguousarray(inputs["L0_ffn_s_gate"]),
        "wq_up": np.ascontiguousarray(inputs["L0_ffn_wq_up"]),
        "s_up": np.ascontiguousarray(inputs["L0_ffn_s_up"]),
        "wq_down": np.ascontiguousarray(inputs["L0_ffn_wq_down"]),
        "s_down": np.ascontiguousarray(inputs["L0_ffn_s_down"]),
        **_host_layouts(inputs),
    }
    nc = build_nc()
    in_maps = [{"x": x[b], **base} for b in range(B)]
    res = run_bass_kernel_spmd(nc, in_maps, core_ids=list(range(B)))
    return np.stack([np.asarray(r["out"], dtype=np.float32) for r in res.results], axis=0)


# version 12
# speedup vs baseline: 1.0517x; 1.0517x over previous
import contextlib
import numpy as np
import concourse.bass as bass
import concourse.mybir as mybir
from concourse.bass_utils import run_bass_kernel_spmd
import ml_dtypes

F32 = mybir.dt.float32
BF16 = mybir.dt.bfloat16
I32 = mybir.dt.int32
AF = mybir.ActivationFunctionType
ALU = mybir.AluOpType
AX = mybir.AxisListType

L = 2048
D = 2048
HID = 5632
NHB = HID // 128
NDB = D // 128
NTT = L // 128
CH = 512
NCH = L // CH
TCH = CH // 128
K_, KQ, M_, H_ = 32, 4, 8, 8
MEM = K_ * H_
ZC = MEM + K_
QD = KQ * H_ * M_ * 2
PI = float(np.pi)
MAGIC = 12582912.0
INV2PI = float(1.0 / (2 * PI))
TWOPI = float(2 * PI)
PI_CLAMP = 3.14159
EPS_LN = 1e-5

CONST_VALUES = [-128.0, 1.0 / D, EPS_LN, 0.5, -0.5, PI / 2, -PI / 2,
                20.0, -20.0, MAGIC, -MAGIC, 1e-4, -1.0, 2.0, 1.0]


class Ctr:
    def __init__(self, h):
        self.h = h
        self.n = 0

    def inc(self, ins, k=1):
        ins.then_inc(self.h, k)
        self.n += k
        return self.n


class DmaPair:
    def __init__(self, c0, c1):
        self.c = [c0, c1]

    def inc(self, i, ins):
        self.c[i % 2].inc(ins, 16)

    def wait(self, eng, i):
        eng.wait_ge(self.c[i % 2].h, 16 * (i // 2 + 1))

    def wait_all(self, eng):
        eng.wait_ge(self.c[0].h, self.c[0].n)
        eng.wait_ge(self.c[1].h, self.c[1].n)


def build_nc(phases=("T", "F1", "A", "F2"), dbg=()):
    nc = bass.Bass("TRN2", target_bir_lowering=False, detect_race_conditions=False)
    g = {"nc": nc, "phases": phases, "dbg_names": dbg}

    x_in = nc.dram_tensor("x", [L, D], F32, kind="ExternalInput").ap()
    out = nc.dram_tensor("out", [L, D], F32, kind="ExternalOutput").ap()
    g["x_in"], g["out"] = x_in, out
    g["wq_g"] = nc.dram_tensor("wq_gate", [HID, D], I32, kind="ExternalInput").ap()
    g["s_g"] = nc.dram_tensor("s_gate", [HID], F32, kind="ExternalInput").ap()
    g["wq_u"] = nc.dram_tensor("wq_up", [HID, D], I32, kind="ExternalInput").ap()
    g["s_u"] = nc.dram_tensor("s_up", [HID], F32, kind="ExternalInput").ap()
    g["wq_d"] = nc.dram_tensor("wq_down", [D, HID], I32, kind="ExternalInput").ap()
    g["s_d"] = nc.dram_tensor("s_down", [D], F32, kind="ExternalInput").ap()
    if "A" in phases:
        g["a_wmem"] = nc.dram_tensor("a_wmem", [D, ZC], F32, kind="ExternalInput").ap()
        g["a_kcol"] = nc.dram_tensor("a_kcol", [4, ZC], F32, kind="ExternalInput").ap()
        g["a_wq"] = nc.dram_tensor("a_wq", [D, QD], F32, kind="ExternalInput").ap()
        g["a_theta"] = nc.dram_tensor("a_theta", [128, MEM * M_], F32, kind="ExternalInput").ap()
        g["a_slopes"] = nc.dram_tensor("a_slopes", [K_, 1], F32, kind="ExternalInput").ap()
        g["a_sscale"] = nc.dram_tensor("a_sscale", [K_, 1], F32, kind="ExternalInput").ap()
        g["a_wblk"] = nc.dram_tensor("a_wblk", [2, 2, 128, 1536], F32, kind="ExternalInput").ap()
        g["a_norm"] = nc.dram_tensor("a_norm", [128, 2], F32, kind="ExternalInput").ap()
        g["a_wout"] = nc.dram_tensor("a_wout", [768, D], F32, kind="ExternalInput").ap()
        g["a_ramp"] = nc.dram_tensor("a_ramp", [K_, L], F32, kind="ExternalInput").ap()

    g["wgu_s"] = nc.dram_tensor("wgu_s", [NHB, 128, 2, NDB, 128], BF16, kind="Internal").ap()
    g["wd_s"] = nc.dram_tensor("wd_s", [NDB, 128, NHB, 128], BF16, kind="Internal").ap()
    g["xmid"] = nc.dram_tensor("xmid", [L, D], F32, kind="Internal").ap()
    g["cr_scr"] = nc.dram_tensor("cr_scr", [1, 4128], F32, kind="Internal").ap()
    g["dbg"] = {}
    _dbg_shapes = {"d_col": ([128, 8], F32), "d_h1": ([128, D], BF16),
                   "d_h1T": ([128, NDB * CH], BF16), "d_tmpT": ([128, 2 * CH], BF16),
                   "d_dn": ([128, NDB * CH], BF16), "d_wgu0": ([128, 2 * NDB * 128], BF16),
                   "d_wd0": ([128, NHB * 128], BF16)}
    for name in dbg:
        shp, dt = _dbg_shapes[name]
        g["dbg"][name] = nc.dram_tensor(name, shp, dt, kind="ExternalOutput").ap()

    g["ident_bf"] = nc.inline_tensor(np.eye(128, dtype=ml_dtypes.bfloat16), "ident_bf").ap()
    g["ident_f"] = nc.inline_tensor(np.eye(128, dtype=np.float32), "ident_f").ap()
    g["tri_bf_d"] = nc.inline_tensor(
        np.triu(np.ones((128, 128))).astype(ml_dtypes.bfloat16), "tri_bf").ap()
    g["ones_row_d"] = nc.inline_tensor(np.ones((1, 128), dtype=np.float32), "ones_row").ap()

    with contextlib.ExitStack() as ctx:
        cvals = sorted(set(float(v) for v in CONST_VALUES))
        cpool = ctx.enter_context(nc.sbuf_tensor("constpool", [128, len(cvals)], F32))
        for i, v in enumerate(cvals):
            nc.const_aps.register(F32, v, cpool[:, i:i + 1])
        g["cpool"], g["cvals"] = cpool, cvals

        sb_id = ctx.enter_context(nc.sbuf_tensor("sb_id", [128, 128], BF16))
        sb_idf = ctx.enter_context(nc.sbuf_tensor("sb_idf", [128, 128], F32))
        sb_tri = ctx.enter_context(nc.sbuf_tensor("sb_tri", [128, 128], BF16))
        sb_ones = ctx.enter_context(nc.sbuf_tensor("sb_ones", [1, 128], F32))
        sb_sg = ctx.enter_context(nc.sbuf_tensor("sb_sg", [128, NHB], F32))
        sb_su = ctx.enter_context(nc.sbuf_tensor("sb_su", [128, NHB], F32))
        sb_sd = ctx.enter_context(nc.sbuf_tensor("sb_sd", [128, NDB], F32))
        sb_sdh = ctx.enter_context(nc.sbuf_tensor("sb_sdh", [128, NDB], F32))
        g.update(sb_id=sb_id, sb_idf=sb_idf, sb_tri=sb_tri, sb_ones=sb_ones,
                 sb_sg=sb_sg, sb_su=sb_su, sb_sd=sb_sd, sb_sdh=sb_sdh)

        sem_names = ["c_ld", "t_dq", "t_tr", "t_ev", "t_ld_2", "a_sld", "vch", "a_sch"]
        if dbg:
            sem_names.append("dbg")
        dma_pairs = ["t_ld", "t_wb"]
        sem_names += [f"f_{s}" for s in (
            "sq", "ln1", "ln2", "h1", "htr", "hev",
            "mmg", "evg", "mmu", "evu", "mmd", "evd", "tr2", "xnew")]
        for p in ("f1", "f2"):
            dma_pairs += [f"{p}_{s}" for s in ("wld", "wdld", "x2", "xst")]
            sem_names += [f"{p}_xld_{k}" for k in range(4)]
        sem_names += [f"a_{s}" for s in (
            "sq", "ln1", "ln2", "h1", "htr", "hev", "mmz", "evz",
            "ztr", "zev", "cv", "pw", "pev", "ktr", "kev",
            "mg", "cs", "csev", "dot", "otr", "oev",
            "ymm", "ysg", "ygt", "omm", "xnew")]
        dma_pairs += ["a_wld", "a_x2", "a_xst", "a_cr"]
        sem_names += [f"a_xld_{k}" for k in range(4)]
        for s in dma_pairs:
            sem_names += [s + "_0", s + "_1"]
        sems = {}
        for s in sem_names:
            sems[s] = Ctr(ctx.enter_context(nc.semaphore(s)))
        for s in dma_pairs:
            sems[s] = DmaPair(sems[s + "_0"], sems[s + "_1"])
        g["sems"] = sems

        with nc.Block() as block:
            @block.gpsimd
            def _(gp):
                c = sems["c_ld"]
                c.inc(gp.dma_start(sb_id[:], g["ident_bf"][:]), 16)
                c.inc(gp.dma_start(sb_idf[:], g["ident_f"][:]), 16)
                c.inc(gp.dma_start(sb_tri[:], g["tri_bf_d"][:]), 16)
                c.inc(gp.dma_start(sb_ones[:], g["ones_row_d"][:]), 16)
                c.inc(gp.dma_start(sb_sg[:], g["s_g"].rearrange("(a p) -> p a", p=128),
                                   allow_slow_non_contiguous=True), 16)
                c.inc(gp.dma_start(sb_su[:], g["s_u"].rearrange("(a p) -> p a", p=128),
                                   allow_slow_non_contiguous=True), 16)
                c.inc(gp.dma_start(sb_sd[:], g["s_d"].rearrange("(a p) -> p a", p=128),
                                   allow_slow_non_contiguous=True), 16)

            @block.vector
            def _(v):
                for i, val in enumerate(cvals):
                    v.memset(cpool[:, i:i + 1], val)
                v.wait_ge(sems["c_ld"].h, sems["c_ld"].n)
                v.tensor_scalar_mul(sb_sdh[:], sb_sd[:], 0.5)

        if "T" in phases:
            emit_transcode(nc, g)
        if "F1" in phases:
            dst = g["xmid"] if ("A" in phases or "F2" in phases) else out
            emit_ffn(nc, g, "f1", x_in, dst)
        if "A" in phases:
            src = g["xmid"] if "F1" in phases else x_in
            dst = g["xmid"] if "F2" in phases else out
            emit_attn(nc, g, src, dst)
        if "F2" in phases:
            src = g["xmid"] if ("F1" in phases or "A" in phases) else x_in
            emit_ffn(nc, g, "f2", src, out)

    return nc


def emit_transcode(nc, g):
    sems = g["sems"]
    t_wb = sems["t_wb"]
    t_ld3 = [sems["t_ld_0"], sems["t_ld_1"], sems["t_ld_2"]]
    t_dq, t_tr, t_ev = sems["t_dq"], sems["t_tr"], sems["t_ev"]
    NJ2 = 3 * (NHB // 2)
    with contextlib.ExitStack() as ctx:
        w_i32 = [ctx.enter_context(nc.sbuf_tensor(f"t_wi{i}", [128, 2 * D], I32)) for i in range(3)]
        w_bf = [ctx.enter_context(nc.sbuf_tensor(f"t_wb{i}", [128, 2 * D], BF16)) for i in range(2)]
        stage = [ctx.enter_context(nc.sbuf_tensor(f"t_st{i}", [128, 2 * D], BF16)) for i in range(2)]
        ps_tr = ctx.enter_context(nc.psum_tensor("t_ps", [128, 2, D], BF16))

        def job_info(job):
            mat, k = divmod(job, NHB // 2)
            return mat, 2 * k

        with nc.Block() as block:
            @block.gpsimd
            def _(gp):
                def load(job):
                    mat, hb = job_info(job)
                    buf = job % 3
                    if job >= 3:
                        gp.wait_ge(t_dq.h, job - 2)
                    if mat == 0:
                        srcl = g["wq_g"][hb * 128:(hb + 2) * 128, :] \
                            .rearrange("(h p) c -> p h c", p=128)
                    elif mat == 1:
                        srcl = g["wq_u"][hb * 128:(hb + 2) * 128, :] \
                            .rearrange("(h p) c -> p h c", p=128)
                    else:
                        srcl = g["wq_d"][:, hb * 128:(hb + 2) * 128] \
                            .rearrange("(a p) c -> p a c", p=128)
                    t_ld3[job % 3].inc(gp.dma_start(w_i32[buf][:], srcl), 16)
                def wb(job):
                    mat, hb = job_info(job)
                    buf = job % 2
                    gp.wait_ge(t_ev.h, 2 * (job + 1))
                    for half in range(2):
                        srcv = stage[buf][:, half * D:(half + 1) * D] \
                            .rearrange("p (a c) -> p a c", c=128)
                        if mat < 2:
                            dstv = g["wgu_s"][hb + half, :, mat, :, :]
                        else:
                            dstv = g["wd_s"][:, :, hb + half, :].transpose([1, 0, 2])
                        ins = gp.dma_start(dstv, srcv)
                        t_wb.c[job % 2].inc(ins, 16)
                load(0)
                load(1)
                load(2)
                for job in range(NJ2):
                    if job + 3 < NJ2:
                        load(job + 3)
                    wb(job)
                t_wb.wait_all(gp)

            @block.scalar
            def _(sc):
                for job in range(NJ2):
                    sc.wait_ge(t_ld3[job % 3].h, 16 * (job // 3 + 1))
                    buf3 = job % 3
                    if job >= 2:
                        sc.wait_ge(t_tr.h, 2 * (job - 1))
                    ins = sc.activation(out=w_bf[job % 2][:], in_=w_i32[buf3][:], func=AF.Identity,
                                        bias=-128.0, scale=1.0)
                    t_dq.inc(ins, 1)

            @block.tensor
            def _(t):
                for job in range(NJ2):
                    mat, hb = job_info(job)
                    t.wait_ge(t_dq.h, job + 1)
                    buf = job % 2
                    for half in range(2):
                        if 2 * job + half >= 2:
                            t.wait_ge(t_ev.h, 2 * job + half - 1)
                        for db in range(NDB):
                            if mat < 2:
                                src = w_bf[buf][:, half * D + db * 128:half * D + (db + 1) * 128]
                            else:
                                src = w_bf[buf][:, db * 256 + half * 128:db * 256 + (half + 1) * 128]
                            ins = t.transpose(out=ps_tr[:, half, db * 128:(db + 1) * 128],
                                              in_=src, identity=g["sb_id"][:])
                        t_tr.inc(ins, 1)

            @block.vector
            def _(v):
                for job in range(NJ2):
                    buf = job % 2
                    for half in range(2):
                        v.wait_ge(t_tr.h, 2 * job + half + 1)
                        if job >= 2:
                            jj = job - 2
                            v.wait_ge(t_wb.c[jj % 2].h, 32 * (jj // 2 + 1))
                        v.tensor_copy(stage[buf][:, half * D:half * D + 1024],
                                      ps_tr[:, half, 0:1024])
                        ins = v.tensor_copy(stage[buf][:, half * D + 1024:half * D + 2048],
                                            ps_tr[:, half, 1024:2048])
                        t_ev.inc(ins, 1)


def emit_ffn(nc, g, p, src_x, dst_x):
    sems = g["sems"]
    _shared = {"sq", "ln1", "ln2", "h1", "htr", "hev", "mmg", "evg", "mmu", "evu",
               "mmd", "evd", "tr2", "xnew"}
    S = lambda s: sems[f"f_{s}"] if s in _shared else sems[f"{p}_{s}"]
    PH = 0 if p == "f1" else 1
    J0 = PH * NTT
    I0 = PH * NCH * NHB
    D0 = PH * NCH * NDB
    xldq = [sems[f"{p}_xld_{k}"] for k in range(4)]

    def xld_wait(eng, j):
        eng.wait_ge(xldq[j % 4].h, 16 * (j // 4 + 1))
    with contextlib.ExitStack() as ctx:
        def sqt(j):
            return x_t[(j - 1) % 4] if j >= 1 else xn[0]
        x_t = [ctx.enter_context(nc.sbuf_tensor(f"{p}_x{i}", [128, D], F32)) for i in range(4)]
        col = ctx.enter_context(nc.sbuf_tensor(f"{p}_col", [128, 8], F32))
        h1 = [ctx.enter_context(nc.sbuf_tensor(f"{p}_h1{i}", [128, D], BF16)) for i in range(4)]
        h1T = [ctx.enter_context(nc.sbuf_tensor(f"{p}_h1T{i}", [128, NDB, CH], BF16)) for i in range(2)]
        wgu = [ctx.enter_context(nc.sbuf_tensor(f"{p}_wgu{i}", [128, 2, NDB, 128], BF16)) for i in range(2)]
        sgb = [ctx.enter_context(nc.sbuf_tensor(f"{p}_sg{i}", [128, CH], BF16)) for i in range(2)]
        tmpT = ctx.enter_context(nc.sbuf_tensor(f"{p}_tmpT", [128, NHB, CH], BF16))
        wd = [ctx.enter_context(nc.sbuf_tensor(f"{p}_wd{i}", [128, NHB, 128], BF16)) for i in range(2)]
        dn = ctx.enter_context(nc.sbuf_tensor(f"{p}_dn", [128, NDB, CH], BF16))
        xo = [ctx.enter_context(nc.sbuf_tensor(f"{p}_xo0", [128, D], F32))]
        xn = [ctx.enter_context(nc.sbuf_tensor(f"{p}_xn{i}", [128, D], F32)) for i in range(2)]
        ps_gu = ctx.enter_context(nc.psum_tensor(f"{p}_psgu", [128, 4, 512], F32))
        ps_tr = ctx.enter_context(nc.psum_tensor(f"{p}_pstr", [128, 2, D], BF16))

        with nc.Block() as block:
            @block.gpsimd
            def _(gp):
                dbg = g["dbg"]
                def xload1(j):
                    if j >= 4:
                        gp.wait_ge(S("ln2").h, 2 * (J0 + j - 3))
                        gp.wait_ge(S("ln1").h, J0 + j - 2)
                    r0 = j * 128
                    xldq[j % 4].inc(gp.dma_start(x_t[j % 4][:], src_x[r0:r0 + 128, :]), 16)
                def xloads(ch):
                    for tt in range(TCH):
                        xload1(ch * TCH + tt)
                def wguloads(ch):
                    if ch >= 1:
                        for tt in range(TCH):
                            xopair(ch - 1, tt)
                    for hb in range(NHB):
                        i = ch * NHB + hb
                        if i >= 2:
                            gp.wait_ge(S("mmu").h, I0 + i - 1)
                        S("wld").inc(i, gp.dma_start(wgu[i % 2][:], g["wgu_s"][hb, :, :, :, :]))
                        if hb % 11 == 10 and ch + 1 < NCH:
                            xload1((ch + 1) * TCH + hb // 11)
                def wdloads(ch):
                    for db in range(NDB):
                        i = ch * NDB + db
                        if i >= 2:
                            gp.wait_ge(S("mmd").h, D0 + i - 1)
                        S("wdld").inc(i, gp.dma_start(wd[i % 2][:], g["wd_s"][db, :, :, :]))
                def xopair(ch, tt):
                    j = ch * TCH + tt
                    if j >= 1:
                        gp.wait_ge(S("xnew").h, J0 + j)
                    r0 = j * 128
                    S("x2").inc(j, gp.dma_start(xo[0][:], src_x[r0:r0 + 128, :]))
                    gp.wait_ge(S("xnew").h, J0 + j + 1)
                    S("xst").inc(j, gp.dma_start(dst_x[r0:r0 + 128, :], xn[j % 2][:]))
                xloads(0)
                wguloads(0)
                wdloads(0)
                for ch in range(1, NCH):
                    wguloads(ch)
                    wdloads(ch)
                for tt in range(TCH):
                    xopair(NCH - 1, tt)
                S("xst").wait_all(gp)
            @block.vector
            def _(v):
                vch = sems["vch"]
                def VC(ins):
                    n = vch.inc(ins, 1)
                    v.wait_ge(vch.h, n)
                def ln_evac(j):
                    ch = j // TCH
                    v.wait_ge(S("htr").h, J0 + j + 1)
                    if ch >= 2:
                        v.wait_ge(S("mmu").h, I0 + (ch - 1) * NHB)
                    for half in range(2):
                        srcv = ps_tr[:, j % 2, half * 1024:(half + 1) * 1024] \
                            .rearrange("p (a c) -> p a c", c=128)
                        dstv = h1T[ch % 2][:, half * 8:(half + 1) * 8,
                                           (j % TCH) * 128:(j % TCH) * 128 + 128]
                        ins = v.tensor_copy(dstv, srcv)
                    S("hev").inc(ins, 1)
                def ln_tile(j):
                    b = j % 4
                    ch = j // TCH
                    xld_wait(v, j)
                    VC(v.tensor_reduce(col[:, 0:1], x_t[b][:], axis=AX.X, op=ALU.add))
                    VC(v.tensor_scalar_mul(col[:, 1:2], col[:, 0:1], 1.0 / D))
                    v.wait_ge(S("sq").h, J0 + j + 1)
                    VC(v.tensor_reduce(col[:, 2:3], sqt(j)[:], axis=AX.X, op=ALU.add))
                    VC(v.tensor_tensor(col[:, 3:4], col[:, 1:2], col[:, 1:2], op=ALU.mult))
                    VC(v.tensor_scalar_mul(col[:, 4:5], col[:, 2:3], 1.0 / D))
                    VC(v.tensor_tensor(col[:, 3:4], col[:, 4:5], col[:, 3:4], op=ALU.subtract))
                    ins = v.tensor_scalar_add(col[:, 3:4], col[:, 3:4], EPS_LN)
                    S("ln1").inc(ins, 1)
                    v.wait_ge(S("ln2").h, 2 * (J0 + j) + 1)
                    VC(v.reciprocal(col[:, 5:6], col[:, 4:5]))
                    ins = v.scalar_tensor_tensor(out=col[:, 6:7], in0=col[:, 1:2], scalar=-1.0,
                                                 in1=col[:, 5:6], op0=ALU.mult, op1=ALU.mult)
                    S("h1").inc(ins, 1)
                for tt in range(TCH):
                    ln_tile(tt)
                    ln_evac(tt)
                for ch in range(NCH):
                    for hb in range(NHB):
                        i = ch * NHB + hb
                        bank_u = (hb % 2) * 2 + 1
                        v.wait_ge(S("mmu").h, I0 + i + 1)
                        v.wait_ge(S("evg").h, I0 + i + 1)
                        ins = v.scalar_tensor_tensor(
                            out=tmpT[:, hb, :], in0=ps_gu[:, bank_u, :],
                            scalar=g["sb_su"][:, hb:hb + 1], in1=sgb[hb % 2][:],
                            op0=ALU.mult, op1=ALU.mult)
                        S("evu").inc(ins, 1)
                        if ch + 1 < NCH and hb % 11 == 10:
                            ln_tile((ch + 1) * TCH + hb // 11)
                    for db in range(NDB):
                        i = ch * NDB + db
                        v.wait_ge(S("mmd").h, D0 + i + 1)
                        ins = v.tensor_scalar_mul(dn[:, db, :], ps_gu[:, db % 4, :],
                                                  g["sb_sdh"][:, db:db + 1])
                        S("evd").inc(ins, 1)
                    for tt in range(TCH):
                        j = ch * TCH + tt
                        b = j % 2
                        v.wait_ge(S("tr2").h, J0 + j + 1)
                        S("x2").wait(v, j)
                        if j >= 2:
                            S("xst").wait(v, j - 2)
                        ins = v.tensor_tensor(out=xn[b][:], in0=ps_tr[:, b, :],
                                              in1=xo[0][:], op=ALU.add)
                        S("xnew").inc(ins, 1)
                    if ch + 1 < NCH:
                        for tt in range(TCH):
                            ln_evac((ch + 1) * TCH + tt)

            @block.scalar
            def _(sc):
                def ln_tile_s(j):
                    b4 = j % 4
                    xld_wait(sc, j)
                    if j >= 1:
                        sc.wait_ge(S("ln1").h, J0 + j)
                    ins = sc.activation(out=sqt(j)[:], in_=x_t[b4][:], func=AF.Square)
                    S("sq").inc(ins, 1)
                    sc.wait_ge(S("ln1").h, J0 + j + 1)
                    ins = sc.activation(out=col[:, 4:5], in_=col[:, 3:4], func=AF.Sqrt)
                    S("ln2").inc(ins, 1)
                    sc.wait_ge(S("h1").h, J0 + j + 1)
                    if j >= 4:
                        sc.wait_ge(S("htr").h, J0 + j - 3)
                    ins = sc.activation(out=h1[b4][:], in_=x_t[b4][:], func=AF.Identity,
                                        scale=col[:, 5:6], bias=col[:, 6:7])
                    S("ln2").inc(ins, 1)
                for tt in range(TCH):
                    ln_tile_s(tt)
                for ch in range(NCH):
                    for hb in range(NHB):
                        i = ch * NHB + hb
                        bank_g = (hb % 2) * 2
                        sc.wait_ge(S("mmg").h, I0 + i + 1)
                        if i >= 2:
                            sc.wait_ge(S("evu").h, I0 + i - 1)
                        ins = sc.activation(out=sgb[hb % 2][:], in_=ps_gu[:, bank_g, :],
                                            func=AF.Silu, scale=g["sb_sg"][:, hb:hb + 1])
                        S("evg").inc(ins, 1)
                        if ch + 1 < NCH and hb % 11 == 10:
                            ln_tile_s((ch + 1) * TCH + hb // 11)
            @block.tensor
            def _(t):
                for ch in range(NCH):
                    for tt in range(TCH):
                        j = ch * TCH + tt
                        b = j % 2
                        t.wait_ge(S("ln2").h, 2 * (J0 + j + 1))
                        if j >= 2:
                            t.wait_ge(S("hev").h, J0 + j - 1)
                        if ch >= 1:
                            t.wait_ge(S("xnew").h, J0 + (ch - 1) * TCH + TCH - 2 + (tt % 2) + 1)
                        for db in range(NDB):
                            ins = t.transpose(out=ps_tr[:, b, db * 128:(db + 1) * 128],
                                              in_=h1[j % 4][:, db * 128:(db + 1) * 128],
                                              identity=g["sb_id"][:])
                        S("htr").inc(ins, 1)
                    for hb in range(NHB):
                        i = ch * NHB + hb
                        bank_g = (hb % 2) * 2
                        bank_u = bank_g + 1
                        S("wld").wait(t, i)
                        if hb == 0:
                            t.wait_ge(S("hev").h, J0 + (ch + 1) * TCH)
                            t.wait_ge(S("evd").h, D0 + ch * NDB)
                        if i >= 2:
                            t.wait_ge(S("evg").h, I0 + i - 1)
                            t.wait_ge(S("evu").h, I0 + i - 1)
                        for db in range(NDB):
                            ins = t.matmul(ps_gu[:, bank_g, :], wgu[i % 2][:, 0, db, :],
                                           h1T[ch % 2][:, db, :], start=(db == 0),
                                           stop=(db == NDB - 1))
                        S("mmg").inc(ins, 1)
                        for db in range(NDB):
                            ins = t.matmul(ps_gu[:, bank_u, :], wgu[i % 2][:, 1, db, :],
                                           h1T[ch % 2][:, db, :], start=(db == 0),
                                           stop=(db == NDB - 1))
                        S("mmu").inc(ins, 1)
                    for db in range(NDB):
                        i = ch * NDB + db
                        S("wdld").wait(t, i)
                        if db == 0:
                            t.wait_ge(S("evu").h, I0 + (ch + 1) * NHB)
                            t.wait_ge(S("evg").h, I0 + (ch + 1) * NHB)
                        if i >= 4:
                            t.wait_ge(S("evd").h, D0 + i - 3)
                        for hb in range(NHB):
                            ins = t.matmul(ps_gu[:, db % 4, :], wd[i % 2][:, hb, :],
                                           tmpT[:, hb, :], start=(hb == 0), stop=(hb == NHB - 1))
                        S("mmd").inc(ins, 1)
                    for tt in range(TCH):
                        j = ch * TCH + tt
                        b = j % 2
                        t.wait_ge(S("evd").h, D0 + (ch + 1) * NDB)
                        if j >= 2:
                            t.wait_ge(S("xnew").h, J0 + j - 1)
                        for db in range(NDB):
                            ins = t.transpose(out=ps_tr[:, b, db * 128:(db + 1) * 128],
                                              in_=dn[:, db, tt * 128:(tt + 1) * 128],
                                              identity=g["sb_id"][:])
                        S("tr2").inc(ins, 1)


def emit_attn(nc, g, src_x, dst_x):
    sems = g["sems"]
    A = lambda s: sems[f"a_{s}"]
    axldq = [sems[f"a_xld_{k}"] for k in range(4)]

    def axld_wait(eng, j):
        eng.wait_ge(axldq[j % 4].h, 16 * (j // 4 + 1))
    NC9 = 9
    CW = [512] * 8 + [32]
    CO = [512 * i for i in range(9)]

    with contextlib.ExitStack() as actx:
        w_mem_bf = actx.enter_context(nc.sbuf_tensor("a_wmemb", [128, NDB, ZC], BF16))
        w_q_bf = actx.enter_context(nc.sbuf_tensor("a_wqb", [128, NDB, QD], BF16))
        wblk_bf = actx.enter_context(nc.sbuf_tensor("a_wblkb", [128, 2, 2, 1536], BF16))
        theta = actx.enter_context(nc.sbuf_tensor("a_theta_sb", [128, MEM * M_], F32))
        norm_sb = actx.enter_context(nc.sbuf_tensor("a_norm_sb", [128, 2], F32))
        acol = actx.enter_context(nc.sbuf_tensor("a_acol", [32, 8], F32))
        kc_sb = actx.enter_context(nc.sbuf_tensor("a_kc", [128, 12], F32))
        oT = [actx.enter_context(nc.sbuf_tensor(f"a_oT{i}", [128, 2, L], BF16)) for i in range(2)]

        with nc.Block() as block:
            @block.gpsimd
            def _(gp):
                c = A("wld")
                c.inc(0, gp.dma_start(
                    w_mem_bf[:], g["a_wmem"].rearrange("(a p) c -> p a c", p=128)))
                c.inc(1, gp.dma_start(
                    w_q_bf[:], g["a_wq"].rearrange("(a p) c -> p a c", p=128)))
                c.inc(2, gp.dma_start(theta[:], g["a_theta"][:]))
                c.inc(3, gp.dma_start(norm_sb[:], g["a_norm"][:]))
                sems["a_sld"].inc(gp.dma_start(acol[:, 0:1], g["a_slopes"][:]), 16)
                sems["a_sld"].inc(gp.dma_start(acol[:, 1:2], g["a_sscale"][:]), 16)
                for ct in range(3):
                    w = 128 if ct < 2 else 32
                    ins = gp.dma_start(kc_sb[0:w, ct * 4:(ct + 1) * 4],
                                       g["a_kcol"][:, ct * 128:ct * 128 + w].transpose([1, 0]),
                                       allow_slow_non_contiguous=True)
                    if ct < 2:
                        sems["a_sld"].inc(ins, 16)
                c.inc(6, ins)
                c.inc(7, gp.dma_start(wblk_bf[:], g["a_wblk"].transpose([2, 0, 1, 3])))
                gp.wait_ge(c.c[0].h, c.c[0].n)
                gp.wait_ge(c.c[1].h, c.c[1].n)
                gp.wait_ge(sems["a_sld"].h, sems["a_sld"].n)

            @block.vector
            def _(v):
                vch = sems["vch"]
                def VC(ins):
                    n = vch.inc(ins, 1)
                    v.wait_ge(vch.h, n)
                v.wait_ge(A("wld").c[0].h, A("wld").c[0].n)
                v.wait_ge(A("wld").c[1].h, A("wld").c[1].n)
                v.wait_ge(sems["a_sld"].h, sems["a_sld"].n)
                for ri in range(2):
                    for kt in range(2):
                        VC(v.tensor_scalar_mul(wblk_bf[:, ri, kt, :], wblk_bf[:, ri, kt, :],
                                               norm_sb[:, kt:kt + 1]))
                v.wait_ge(A("sq").h, 1)
                VC(v.tensor_scalar_max(acol[0:32, 5:6], acol[0:32, 0:1], 0.0))
                VC(v.tensor_add(acol[0:32, 6:7], acol[0:32, 5:6], acol[0:32, 4:5]))
                ins = v.tensor_scalar_mul(acol[0:32, 7:8], acol[0:32, 6:7], -1.0)
                A("cv").inc(ins, 1)

            @block.scalar
            def _(sc):
                sch = sems["a_sch"]
                def SC(ins):
                    n = sch.inc(ins, 1)
                    sc.wait_ge(sch.h, n)
                sc.wait_ge(sems["a_sld"].h, 64)
                SC(sc.activation(out=acol[0:32, 2:3], in_=acol[0:32, 0:1], func=AF.Abs))
                SC(sc.activation(out=acol[0:32, 3:4], in_=acol[0:32, 2:3], func=AF.Exp,
                                 scale=-1.0))
                ins = sc.activation(out=acol[0:32, 4:5], in_=acol[0:32, 3:4], func=AF.Ln,
                                    bias=1.0)
                A("sq").inc(ins, 1)

        with contextlib.ExitStack() as zctx:
            k_val = zctx.enter_context(nc.sbuf_tensor("a_kval", [128, NTT, MEM], F32))
            p_w = zctx.enter_context(nc.sbuf_tensor("a_pw", [128, NTT, K_], F32))
            q_sb = zctx.enter_context(nc.sbuf_tensor("a_qsb", [128, NTT, QD], BF16))
            zscope = contextlib.ExitStack()
            z_sb = zscope.enter_context(nc.sbuf_tensor("a_zsb", [128, NTT, ZC], F32))

            with contextlib.ExitStack() as ctx:
                x_t = [ctx.enter_context(nc.sbuf_tensor(f"a_x{i}", [128, D], F32)) for i in range(4)]
                sq_dump = [ctx.enter_context(nc.sbuf_tensor(f"a_sqd{i}", [128, D], F32)) for i in range(2)]
                col = ctx.enter_context(nc.sbuf_tensor("a_col", [128, 2, 8], F32))
                h1 = [ctx.enter_context(nc.sbuf_tensor(f"a_h1{i}", [128, D], BF16)) for i in range(4)]
                h2T = ctx.enter_context(nc.sbuf_tensor("a_h2T", [128, NDB, 128], BF16))
                ps_tr = ctx.enter_context(nc.psum_tensor("a_pstr", [128, 2, D], BF16))
                ps_z = ctx.enter_context(nc.psum_tensor("a_psz", [128, 2, 512], F32))
                ps_q = ctx.enter_context(nc.psum_tensor("a_psq", [128, 2, 512], F32))

                with nc.Block() as block:
                    @block.gpsimd
                    def _(gp):
                        for j in range(NTT):
                            if j >= 4:
                                gp.wait_ge(A("ln2").h, 2 * (j - 3))
                                gp.wait_ge(A("ln1").h, j - 2)
                            axldq[j % 4].inc(gp.dma_start(
                                x_t[j % 4][:], src_x[j * 128:(j + 1) * 128, :]), 16)
                        gp.wait_ge(A("evz").h, NTT)

                    @block.vector
                    def _(v):
                        vch = sems["vch"]
                        def VC(ins):
                            n = vch.inc(ins, 1)
                            v.wait_ge(vch.h, n)
                        for j in range(NTT):
                            b = j % 2
                            cl = col[:, j % 2, :]
                            axld_wait(v, j)
                            if j >= 2:
                                v.wait_ge(A("ln2").h, 2 * (j - 1))
                            VC(v.tensor_reduce(cl[:, 0:1], x_t[j % 4][:], axis=AX.X, op=ALU.add))
                            VC(v.tensor_scalar_mul(cl[:, 1:2], cl[:, 0:1], 1.0 / D))
                            v.wait_ge(A("sq").h, 1 + j + 1)
                            VC(v.tensor_reduce(cl[:, 2:3], sq_dump[j % 2][:], axis=AX.X, op=ALU.add))
                            VC(v.tensor_tensor(cl[:, 3:4], cl[:, 1:2], cl[:, 1:2], op=ALU.mult))
                            VC(v.tensor_scalar_mul(cl[:, 4:5], cl[:, 2:3], 1.0 / D))
                            VC(v.tensor_tensor(cl[:, 3:4], cl[:, 4:5], cl[:, 3:4], op=ALU.subtract))
                            ins = v.tensor_scalar_add(cl[:, 3:4], cl[:, 3:4], EPS_LN)
                            A("ln1").inc(ins, 1)
                            v.wait_ge(A("ln2").h, 2 * j + 1)
                            VC(v.reciprocal(cl[:, 5:6], cl[:, 4:5]))
                            ins = v.scalar_tensor_tensor(out=cl[:, 6:7], in0=cl[:, 1:2], scalar=-1.0,
                                                         in1=cl[:, 5:6], op0=ALU.mult, op1=ALU.mult)
                            A("h1").inc(ins, 1)
                            v.wait_ge(A("htr").h, j + 1)
                            for half in range(2):
                                srcv = ps_tr[:, b, half * 1024:(half + 1) * 1024] \
                                    .rearrange("p (a c) -> p a c", c=128)
                                dstv = h2T[:, half * 8:(half + 1) * 8, :]
                                ins = v.tensor_copy(dstv, srcv)
                            A("hev").inc(ins, 1)
                            v.wait_ge(A("mmz").h, 2 * (j + 1))
                            v.tensor_copy(z_sb[:, j, :], ps_z[:, b, 0:ZC])
                            ins = v.tensor_copy(q_sb[:, j, :], ps_q[:, b, :])
                            A("evz").inc(ins, 1)

                    @block.scalar
                    def _(sc):
                        def square(j):
                            axld_wait(sc, j)
                            if j >= 2:
                                sc.wait_ge(A("ln1").h, j - 1)
                            ins = sc.activation(out=sq_dump[j % 2][:], in_=x_t[j % 4][:],
                                                func=AF.Square)
                            A("sq").inc(ins, 1)
                        square(0)
                        for j in range(NTT):
                            cl = col[:, j % 2, :]
                            sc.wait_ge(A("ln1").h, j + 1)
                            ins = sc.activation(out=cl[:, 4:5], in_=cl[:, 3:4], func=AF.Sqrt)
                            A("ln2").inc(ins, 1)
                            if j + 1 < NTT:
                                square(j + 1)
                            sc.wait_ge(A("h1").h, j + 1)
                            if j >= 4:
                                sc.wait_ge(A("htr").h, j - 3)
                            ins = sc.activation(out=h1[j % 4][:], in_=x_t[j % 4][:], func=AF.Identity,
                                                scale=cl[:, 5:6], bias=cl[:, 6:7])
                            A("ln2").inc(ins, 1)

                    @block.tensor
                    def _(t):
                        for j in range(NTT):
                            b = j % 2
                            t.wait_ge(A("ln2").h, 2 * (j + 1))
                            if j >= 2:
                                t.wait_ge(A("hev").h, j - 1)
                            for db in range(NDB):
                                ins = t.transpose(out=ps_tr[:, b, db * 128:(db + 1) * 128],
                                                  in_=h1[j % 4][:, db * 128:(db + 1) * 128],
                                                  identity=g["sb_id"][:])
                            A("htr").inc(ins, 1)
                            t.wait_ge(A("hev").h, j + 1)
                            if j >= 2:
                                t.wait_ge(A("evz").h, j - 1)
                            for db in range(NDB):
                                ins = t.matmul(ps_z[:, b, 0:ZC], h2T[:, db, :],
                                               w_mem_bf[:, db, :], start=(db == 0),
                                               stop=(db == NDB - 1))
                            A("mmz").inc(ins, 1)
                            for db in range(NDB):
                                ins = t.matmul(ps_q[:, b, :], h2T[:, db, :],
                                               w_q_bf[:, db, :], start=(db == 0),
                                               stop=(db == NDB - 1))
                            A("mmz").inc(ins, 1)

            with contextlib.ExitStack() as ctx:
                zT = ctx.enter_context(nc.sbuf_tensor("a_zT", [128, 3, 2052], F32))
                zc = ctx.enter_context(nc.sbuf_tensor("a_zc", [128, 3, L], F32))
                ramp = ctx.enter_context(nc.sbuf_tensor("a_ramp_sb", [32, L], F32))
                pwT = ctx.enter_context(nc.sbuf_tensor("a_pwT", [32, L], F32))
                ps_f = ctx.enter_context(nc.psum_tensor("a_psf", [128, 2, D], F32))

                with nc.Block() as block:
                    @block.gpsimd
                    def _(gp):
                        nr = sems["a_sld"].inc(gp.dma_start(ramp[:], g["a_ramp"][:]), 16)
                        gp.wait_ge(A("kev").h, 2 * NTT)

                    @block.vector
                    def _(v):
                        vch = sems["vch"]
                        def VC(ins):
                            n = vch.inc(ins, 1)
                            v.wait_ge(vch.h, n)
                        for ct in range(3):
                            v.memset(zT[:, ct, 0:4], 0.0)
                        ins = v.memset(pwT[0:1, 0:1], 0.0)
                        A("ztr").inc(ins, 1)
                        for ct in range(3):
                            v.wait_ge(A("ztr").h, ct + 2)
                            w = 128 if ct < 2 else 32
                            ins = v.tensor_copy(zT[0:w, ct, 4:2052], ps_f[0:w, ct % 2, :])
                            A("zev").inc(ins, 1)
                        for ct in range(3):
                            w = 128 if ct < 2 else 32
                            v.wait_ge(A("zev").h, ct + 1)
                            v.tensor_scalar_mul(zc[0:w, ct, :], zT[0:w, ct, 4:2052],
                                                kc_sb[0:w, ct * 4 + 3:ct * 4 + 4])
                            for jj in (2, 1, 0):
                                v.scalar_tensor_tensor(
                                    out=zc[0:w, ct, :], in0=zT[0:w, ct, jj + 1:jj + 1 + 2048],
                                    scalar=kc_sb[0:w, ct * 4 + jj:ct * 4 + jj + 1],
                                    in1=zc[0:w, ct, :], op0=ALU.mult, op1=ALU.add)
                            ins = v.tensor_copy(zc[0:1, ct, 0:1], zc[0:1, ct, 0:1])
                            A("cv").inc(ins, 1)
                        v.wait_ge(A("cv").h, 4)
                        v.wait_ge(sems["a_sld"].h, sems["a_sld"].n)
                        VC(v.tensor_scalar(pwT[:], zc[0:32, 2, :], acol[0:32, 1:2], 20.0,
                                           op0=ALU.mult, op1=ALU.min))
                        VC(v.tensor_scalar_max(pwT[:], pwT[:], -20.0))
                        VC(v.tensor_scalar_mul(ramp[:], ramp[:], acol[0:32, 7:8]))
                        ins = v.tensor_tensor(pwT[:], pwT[:], ramp[:], op=ALU.add)
                        A("pw").inc(ins, 1)
                        for j in range(NTT):
                            v.wait_ge(A("ktr").h, j + 1)
                            v.tensor_copy(k_val[:, j, :], ps_f[:, j % 2, 0:MEM])
                            ins = v.tensor_copy(p_w[:, j, :], ps_f[:, j % 2, MEM:MEM + K_])
                            A("kev").inc(ins, 2)

                    @block.scalar
                    def _(sc):
                        sc.wait_ge(A("pw").h, 1)
                        ins = sc.activation(out=pwT[:], in_=pwT[:], func=AF.Exp)
                        A("pw").inc(ins, 1)

                    @block.tensor
                    def _(t):
                        for ct in range(3):
                            w = 128 if ct < 2 else 32
                            if ct >= 2:
                                t.wait_ge(A("zev").h, ct - 1)
                            for j in range(NTT):
                                ins = t.transpose(
                                    out=ps_f[0:w, ct % 2, j * 128:(j + 1) * 128],
                                    in_=z_sb[:, j, ct * 128:ct * 128 + w],
                                    identity=g["sb_idf"][:])
                            A("ztr").inc(ins, 1)
                        t.wait_ge(A("pw").h, 2)
                        t.wait_ge(A("cv").h, 4)
                        for j in range(NTT):
                            if j >= 2:
                                t.wait_ge(A("kev").h, 2 * (j - 1))
                            for ct in range(2):
                                t.transpose(out=ps_f[:, j % 2, ct * 128:(ct + 1) * 128],
                                            in_=zc[:, ct, j * 128:(j + 1) * 128],
                                            identity=g["sb_idf"][:])
                            ins = t.transpose(out=ps_f[:, j % 2, MEM:MEM + K_],
                                              in_=pwT[0:32, j * 128:(j + 1) * 128],
                                              identity=g["sb_idf"][0:32, 0:32])
                            A("ktr").inc(ins, 1)

            zscope.close()
            with contextlib.ExitStack() as ctx:
                phi = ctx.enter_context(nc.sbuf_tensor("a_phi", [128, 2048], F32))
                trg = ctx.enter_context(nc.sbuf_tensor("a_trg", [128, 2048], F32))
                prod = ctx.enter_context(nc.sbuf_tensor("a_prod", [128, 2048], F32))
                prod2 = ctx.enter_context(nc.sbuf_tensor("a_prod2", [128, 2048], F32))
                merged = [ctx.enter_context(nc.sbuf_tensor(f"a_mg{i}", [128, 4128], BF16))
                          for i in range(2)]
                cs = [ctx.enter_context(nc.sbuf_tensor(f"a_cs{i}", [128, 4128], F32))
                      for i in range(2)]
                qf = ctx.enter_context(nc.sbuf_tensor("a_qf", [128, QD], F32))
                ore = ctx.enter_context(nc.sbuf_tensor("a_ore", [128, MEM], F32))
                oim = ctx.enter_context(nc.sbuf_tensor("a_oim", [128, MEM], F32))
                ot1 = ctx.enter_context(nc.sbuf_tensor("a_ot1", [128, 2 * K_], F32))
                obf = ctx.enter_context(nc.sbuf_tensor("a_obf", [128, 2, MEM], BF16))
                carry = ctx.enter_context(nc.sbuf_tensor("a_carry", [128, 4128], F32))
                ps_cs = ctx.enter_context(nc.psum_tensor("a_pscs", [128, 2, 512], F32))
                ps_o = ctx.enter_context(nc.psum_tensor("a_pso", [128, 2, MEM], BF16))

                def qviews(b):
                    vs = []
                    for kq in range(KQ):
                        base = kq * 128
                        qre_v = qf[:, base:base + 127:2].rearrange("p (h m) -> p h m", m=M_) \
                            .unsqueeze(1).broadcast_to((128, 8, 8, M_))
                        qim_v = qf[:, base + 1:base + 128:2].rearrange("p (h m) -> p h m", m=M_) \
                            .unsqueeze(1).broadcast_to((128, 8, 8, M_))
                        csr = cs[b][:, K_ + kq * 512:K_ + (kq + 1) * 512] \
                            .rearrange("p (r h m) -> p r h m", h=8, m=M_)
                        csi = cs[b][:, K_ + 2048 + kq * 512:K_ + 2048 + (kq + 1) * 512] \
                            .rearrange("p (r h m) -> p r h m", h=8, m=M_)
                        pr = prod[:, kq * 512:(kq + 1) * 512].rearrange(
                            "p (r h m) -> p r h m", h=8, m=M_)
                        pr2 = prod2[:, kq * 512:(kq + 1) * 512].rearrange(
                            "p (r h m) -> p r h m", h=8, m=M_)
                        vs.append((qre_v, qim_v, csr, csi, pr, pr2))
                    return vs

                with nc.Block() as block:
                    @block.gpsimd
                    def _(gp):
                        cr0, cr1 = A("cr").c
                        for j in range(NTT - 1):
                            gp.wait_ge(A("csev").h, NC9 * (j + 1))
                            if j >= 1:
                                gp.wait_ge(cr1.h, 16 * j)
                            cr0.inc(gp.dma_start(g["cr_scr"], cs[j % 2][127:128, :]), 16)
                            gp.wait_ge(cr0.h, 16 * (j + 1))
                            cr1.inc(gp.dma_start(
                                carry[:], g["cr_scr"][0:1, :].broadcast_to((128, 4128))), 16)
                        gp.wait_ge(A("oev").h, 2 * NTT)

                    @block.vector
                    def _(v):
                        vch = sems["vch"]
                        def VC(ins):
                            n = vch.inc(ins, 1)
                            v.wait_ge(vch.h, n)
                        def trig_pre(j):
                            kv = k_val[:, j, :].unsqueeze(2).broadcast_to((128, MEM, M_))
                            th3 = theta[:].rearrange("p (a c) -> p a c", c=M_)
                            ph3 = phi[:].rearrange("p (a c) -> p a c", c=M_)
                            v.tensor_tensor(out=ph3, in0=kv, in1=th3, op=ALU.mult)
                            v.tensor_scalar(prod[:], phi[:], INV2PI, MAGIC,
                                            op0=ALU.mult, op1=ALU.add)
                            v.tensor_scalar_sub(prod[:], prod[:], MAGIC)
                            v.scalar_tensor_tensor(out=trg[:], in0=prod[:], scalar=TWOPI,
                                                   in1=phi[:], op0=ALU.mult, op1=ALU.subtract)
                            ins = v.tensor_scalar(trg[:], trg[:], -PI_CLAMP, PI_CLAMP,
                                                  op0=ALU.max, op1=ALU.min)
                            A("mg").inc(ins, 1)
                        def merged_build(j):
                            b = j % 2
                            v.wait_ge(A("pev").h, 2 * j + 2)
                            if j >= 2:
                                v.wait_ge(A("cs").h, NC9 * (j - 1))
                            pw3 = p_w[:, j, :].unsqueeze(2).broadcast_to((128, K_, 64))
                            v.tensor_copy(merged[b][:, 0:K_], p_w[:, j, :])
                            mg3 = merged[b][:, K_:K_ + 2048].rearrange("p (a c) -> p a c", c=64)
                            v.tensor_tensor(out=mg3, in0=phi[:].rearrange("p (a c) -> p a c", c=64),
                                            in1=pw3, op=ALU.mult)
                            mg4 = merged[b][:, K_ + 2048:].rearrange("p (a c) -> p a c", c=64)
                            ins = v.tensor_tensor(out=mg4,
                                                  in0=trg[:].rearrange("p (a c) -> p a c", c=64),
                                                  in1=pw3, op=ALU.mult)
                            A("mg").inc(ins, 1)
                        trig_pre(0)
                        merged_build(0)
                        for j in range(NTT):
                            b = j % 2
                            for c9 in range(NC9):
                                v.wait_ge(A("cs").h, NC9 * j + c9 + 1)
                                if j == 0:
                                    ins = v.tensor_copy(cs[b][:, CO[c9]:CO[c9] + CW[c9]],
                                                        ps_cs[:, c9 % 2, 0:CW[c9]])
                                else:
                                    if c9 == 0:
                                        v.wait_ge(A("cr").c[1].h, 16 * j)
                                    ins = v.tensor_tensor(
                                        out=cs[b][:, CO[c9]:CO[c9] + CW[c9]],
                                        in0=ps_cs[:, c9 % 2, 0:CW[c9]],
                                        in1=carry[:, CO[c9]:CO[c9] + CW[c9]], op=ALU.add)
                                A("csev").inc(ins, 1)
                            if j + 1 < NTT:
                                trig_pre(j + 1)
                            VC(v.tensor_copy(qf[:], q_sb[:, j, :]))
                            vs = qviews(b)
                            for (qre_v, qim_v, csr, csi, pr, pr2) in vs:
                                v.tensor_tensor(out=pr, in0=csr, in1=qre_v, op=ALU.mult)
                                v.tensor_tensor(out=pr2, in0=csi, in1=qim_v, op=ALU.mult)
                            VC(v.tensor_add(prod[:], prod[:], prod2[:]))
                            for kq in range(KQ):
                                pr3 = prod[:, kq * 512:(kq + 1) * 512].rearrange(
                                    "p (a m) -> p a m", m=M_)
                                ins = v.tensor_reduce(ore[:, kq * 64:(kq + 1) * 64], pr3,
                                                      axis=AX.X, op=ALU.add)
                            VC(ins)
                            for (qre_v, qim_v, csr, csi, pr, pr2) in vs:
                                v.tensor_tensor(out=pr, in0=csi, in1=qre_v, op=ALU.mult)
                                v.tensor_tensor(out=pr2, in0=csr, in1=qim_v, op=ALU.mult)
                            VC(v.tensor_sub(prod[:], prod[:], prod2[:]))
                            for kq in range(KQ):
                                pr3 = prod[:, kq * 512:(kq + 1) * 512].rearrange(
                                    "p (a m) -> p a m", m=M_)
                                ins = v.tensor_reduce(oim[:, kq * 64:(kq + 1) * 64], pr3,
                                                      axis=AX.X, op=ALU.add)
                            VC(ins)
                            if j + 1 < NTT:
                                merged_build(j + 1)
                            VC(v.tensor_scalar_max(ot1[:, 0:K_], cs[b][:, 0:K_], 1e-4))
                            VC(v.reciprocal(ot1[:, K_:2 * K_], ot1[:, 0:K_]))
                            idv = ot1[:, K_:2 * K_].unsqueeze(2).broadcast_to((128, K_, H_))
                            for ri, srcb in ((0, ore), (1, oim)):
                                VC(v.tensor_tensor(
                                    out=obf[:, ri, :].rearrange("p (a c) -> p a c", c=H_),
                                    in0=srcb[:].rearrange("p (a c) -> p a c", c=H_),
                                    in1=idv, op=ALU.mult))
                            A("dot").inc(v.tensor_copy(ot1[0:1, 0:1], ot1[0:1, 0:1]), 1)
                            v.wait_ge(A("otr").h, j + 1)
                            for ri in range(2):
                                srcv = ps_o[:, ri, :].rearrange("p (a c) -> p a c", c=128)
                                dstv = oT[ri][:, :, j * 128:(j + 1) * 128]
                                ins = v.tensor_copy(dstv, srcv)
                            A("oev").inc(ins, 2)

                    @block.scalar
                    def _(sc):
                        sch = sems["a_sch"]
                        def SC(ins):
                            n = sch.inc(ins, 1)
                            sc.wait_ge(sch.h, n)
                        for j in range(NTT):
                            sc.wait_ge(A("mg").h, 2 * j + 1)
                            if j >= 1:
                                sc.wait_ge(A("mg").h, 2 * j)
                            SC(sc.activation(out=phi[:], in_=trg[:], func=AF.Abs))
                            A("pev").inc(sc.activation(out=phi[:], in_=phi[:], func=AF.Sin,
                                                       bias=PI / 2, scale=-1.0), 1)
                            ins = sc.activation(out=trg[:], in_=trg[:], func=AF.Sin, scale=-1.0)
                            A("pev").inc(ins, 1)

                    @block.tensor
                    def _(t):
                        def otr(j):
                            t.wait_ge(A("dot").h, j + 1)
                            if j >= 2:
                                t.wait_ge(A("oev").h, 2 * (j - 1))
                            for ri in range(2):
                                for kt in range(2):
                                    ins = t.transpose(out=ps_o[:, ri, kt * 128:(kt + 1) * 128],
                                                      in_=obf[:, ri, kt * 128:(kt + 1) * 128],
                                                      identity=g["sb_id"][:])
                            A("otr").inc(ins, 1)
                        for j in range(NTT):
                            b = j % 2
                            if j >= 1:
                                otr(j - 1)
                            t.wait_ge(A("mg").h, 2 * (j + 1))
                            for c9 in range(NC9):
                                if NC9 * j + c9 >= 2:
                                    t.wait_ge(A("csev").h, NC9 * j + c9 - 1)
                                ins = t.matmul(ps_cs[:, c9 % 2, 0:CW[c9]], g["sb_tri"][:],
                                               merged[b][:, CO[c9]:CO[c9] + CW[c9]],
                                               start=True, stop=True)
                                A("cs").inc(ins, 1)
                        otr(NTT - 1)

        with contextlib.ExitStack() as ctx:
            w_out_bf = ctx.enter_context(nc.sbuf_tensor("a_woutb", [128, 6, D], BF16))
            sg_sb = ctx.enter_context(nc.sbuf_tensor("a_sg", [128, 6, 512], BF16))
            gT = ctx.enter_context(nc.sbuf_tensor("a_gT", [128, 6, 512], BF16))
            xo = [ctx.enter_context(nc.sbuf_tensor(f"a_xo{i}", [128, D], F32)) for i in range(2)]
            xn = [ctx.enter_context(nc.sbuf_tensor(f"a_xn{i}", [128, D], F32)) for i in range(2)]
            ps_y = ctx.enter_context(nc.psum_tensor("a_psy", [128, 2, 512], F32))
            ps_x = ctx.enter_context(nc.psum_tensor("a_psx", [128, D], F32))

            yorder = list(range(6, 12)) + list(range(0, 6))
            ycons = []

            wout_n = [0]
            with nc.Block() as block:
                @block.gpsimd
                def _(gp):
                    wout_n[0] = A("wld").c[0].inc(gp.dma_start(
                        w_out_bf[:], g["a_wout"].rearrange("(a p) c -> p a c", p=128)), 16)
                    for j in range(NTT):
                        if j >= 2:
                            gp.wait_ge(A("xnew").h, j - 1)
                        A("x2").inc(j, gp.dma_start(xo[j % 2][:], src_x[j * 128:(j + 1) * 128, :]))
                        gp.wait_ge(A("xnew").h, j + 1)
                        A("xst").inc(j, gp.dma_start(dst_x[j * 128:(j + 1) * 128, :], xn[j % 2][:]))
                    A("xst").wait_all(gp)

                @block.vector
                def _(v):
                    for cq in range(4):
                        for mi, mb in enumerate(range(0, 6)):
                            i = cq * 12 + 6 + mi
                            v.wait_ge(A("ymm").h, i + 1)
                            v.wait_ge(A("ysg").h, cq * 6 + mb + 1)
                            if cq >= 1:
                                v.wait_ge(A("omm").h, 4 * cq)
                            ins = v.tensor_tensor(out=gT[:, mb, :], in0=ps_y[:, i % 2, :],
                                                  in1=sg_sb[:, mb, :], op=ALU.mult)
                            A("ygt").inc(ins, 1)
                        for tt in range(4):
                            j = cq * 4 + tt
                            bb = j % 2
                            v.wait_ge(A("omm").h, j + 1)
                            A("x2").wait(v, j)
                            if j >= 2:
                                A("xst").wait(v, j - 2)
                            ins = v.tensor_tensor(out=xn[bb][:], in0=ps_x[:], in1=xo[bb][:],
                                                  op=ALU.add)
                            A("xnew").inc(ins, 1)

                @block.scalar
                def _(sc):
                    for cq in range(4):
                        for gi in range(6):
                            i = cq * 12 + gi
                            sc.wait_ge(A("ymm").h, i + 1)
                            if cq >= 1:
                                sc.wait_ge(A("ygt").h, (cq - 1) * 6 + gi + 1)
                            ins = sc.activation(out=sg_sb[:, gi, :], in_=ps_y[:, i % 2, :],
                                                func=AF.Silu)
                            A("ysg").inc(ins, 1)

                @block.tensor
                def _(t):
                    for cq in range(4):
                        for idx, mb in enumerate(yorder):
                            i = cq * 12 + idx
                            if mb < 6:
                                kt = 1 if mb >= 3 else 0
                                col0 = mb * 128
                            else:
                                kt = 1 if mb >= 9 else 0
                                col0 = 768 + (mb - 6) * 128
                            if i >= 2:
                                s_, n_ = ycons[i - 2]
                                t.wait_ge(s_.h, n_)
                            for ri in range(2):
                                ins = t.matmul(ps_y[:, i % 2, :],
                                               wblk_bf[:, ri, kt, col0:col0 + 128],
                                               oT[ri][:, kt, cq * 512:(cq + 1) * 512],
                                               start=(ri == 0), stop=(ri == 1))
                            A("ymm").inc(ins, 1)
                            if mb >= 6:
                                ycons.append((A("ysg"), cq * 6 + (mb - 6) + 1))
                            else:
                                ycons.append((A("ygt"), cq * 6 + mb + 1))
                        for tt in range(4):
                            j = cq * 4 + tt
                            if j == 0:
                                t.wait_ge(A("wld").c[0].h, wout_n[0])
                            t.wait_ge(A("ygt").h, cq * 6 + 6)
                            if j >= 1:
                                t.wait_ge(A("xnew").h, j)
                            for cb in range(6):
                                for nn_ in range(4):
                                    ins = t.matmul(ps_x[:, nn_ * 512:(nn_ + 1) * 512],
                                                   gT[:, cb, tt * 128:(tt + 1) * 128],
                                                   w_out_bf[:, cb, nn_ * 512:(nn_ + 1) * 512],
                                                   start=(cb == 0), stop=(cb == 5))
                            A("omm").inc(ins, 1)


def _host_layouts(inputs):
    wre = np.asarray(inputs["L0_attn_W_re"])
    wim = np.asarray(inputs["L0_attn_W_im"])
    blk = np.zeros((2, 2, 128, 1536), dtype=np.float32)
    for idx, w in enumerate((wre, wim)):
        for k in range(K_):
            kt, khl = divmod(k * H_, 128)
            rows = slice(khl, khl + H_)
            blk[idx, kt, rows, k * 24:(k + 1) * 24] = w[k, :, 0:24]
            blk[idx, kt, rows, 768 + k * 24:768 + (k + 1) * 24] = w[k, :, 24:48]
    theta = np.asarray(inputs["L0_attn_theta"]).reshape(1, MEM * M_)
    norm = np.asarray(inputs["L0_attn_norm_scale"]).reshape(1, MEM)
    slopes = np.concatenate([np.asarray(inputs["L0_attn_decay_slopes"]),
                             np.asarray(inputs["L0_attn_anchor_slopes"])]).reshape(K_, 1).astype(np.float32)
    t_ramp = np.arange(L, dtype=np.float32)
    ramp = np.zeros((K_, L), dtype=np.float32)
    ramp[0:28, :] = (L - 1) - t_ramp[None, :]
    ramp[28:32, :] = t_ramp[None, :]
    return {
        "a_wmem": np.ascontiguousarray(inputs["L0_attn_W_mem"], dtype=np.float32),
        "a_kcol": np.ascontiguousarray(np.asarray(inputs["L0_attn_conv_k"]).reshape(4, ZC)),
        "a_wq": np.ascontiguousarray(inputs["L0_attn_W_q"], dtype=np.float32),
        "a_theta": np.ascontiguousarray(np.broadcast_to(theta, (128, MEM * M_))),
        "a_slopes": slopes,
        "a_sscale": np.ascontiguousarray(
            np.asarray(inputs["L0_attn_score_scale"]).reshape(K_, 1), dtype=np.float32),
        "a_wblk": blk,
        "a_norm": np.ascontiguousarray(norm.reshape(2, 128).T),
        "a_wout": np.ascontiguousarray(inputs["L0_attn_W_out"], dtype=np.float32),
        "a_ramp": ramp,
    }


def kernel(**inputs):
    x = np.ascontiguousarray(np.asarray(inputs["L0_attn_x"], dtype=np.float32))
    B = x.shape[0]
    base = {
        "wq_gate": np.ascontiguousarray(inputs["L0_ffn_wq_gate"]),
        "s_gate": np.ascontiguousarray(inputs["L0_ffn_s_gate"]),
        "wq_up": np.ascontiguousarray(inputs["L0_ffn_wq_up"]),
        "s_up": np.ascontiguousarray(inputs["L0_ffn_s_up"]),
        "wq_down": np.ascontiguousarray(inputs["L0_ffn_wq_down"]),
        "s_down": np.ascontiguousarray(inputs["L0_ffn_s_down"]),
        **_host_layouts(inputs),
    }
    nc = build_nc()
    in_maps = [{"x": x[b], **base} for b in range(B)]
    res = run_bass_kernel_spmd(nc, in_maps, core_ids=list(range(B)))
    return np.stack([np.asarray(r["out"], dtype=np.float32) for r in res.results], axis=0)


# version 13
# speedup vs baseline: 1.0532x; 1.0015x over previous
import contextlib
import numpy as np
import concourse.bass as bass
import concourse.mybir as mybir
from concourse.bass_utils import run_bass_kernel_spmd
import ml_dtypes

F32 = mybir.dt.float32
BF16 = mybir.dt.bfloat16
I32 = mybir.dt.int32
AF = mybir.ActivationFunctionType
ALU = mybir.AluOpType
AX = mybir.AxisListType

L = 2048
D = 2048
HID = 5632
NHB = HID // 128
NDB = D // 128
NTT = L // 128
CH = 512
NCH = L // CH
TCH = CH // 128
K_, KQ, M_, H_ = 32, 4, 8, 8
MEM = K_ * H_
ZC = MEM + K_
QD = KQ * H_ * M_ * 2
PI = float(np.pi)
MAGIC = 12582912.0
INV2PI = float(1.0 / (2 * PI))
TWOPI = float(2 * PI)
PI_CLAMP = 3.14159
EPS_LN = 1e-5

CONST_VALUES = [-128.0, 1.0 / D, EPS_LN, 0.5, -0.5, PI / 2, -PI / 2,
                20.0, -20.0, MAGIC, -MAGIC, 1e-4, -1.0, 2.0, 1.0]


class Ctr:
    def __init__(self, h):
        self.h = h
        self.n = 0

    def inc(self, ins, k=1):
        ins.then_inc(self.h, k)
        self.n += k
        return self.n


class DmaPair:
    def __init__(self, c0, c1):
        self.c = [c0, c1]

    def inc(self, i, ins):
        self.c[i % 2].inc(ins, 16)

    def wait(self, eng, i):
        eng.wait_ge(self.c[i % 2].h, 16 * (i // 2 + 1))

    def wait_all(self, eng):
        eng.wait_ge(self.c[0].h, self.c[0].n)
        eng.wait_ge(self.c[1].h, self.c[1].n)


def build_nc(phases=("T", "F1", "A", "F2"), dbg=()):
    nc = bass.Bass("TRN2", target_bir_lowering=False, detect_race_conditions=False)
    g = {"nc": nc, "phases": phases, "dbg_names": dbg}

    x_in = nc.dram_tensor("x", [L, D], F32, kind="ExternalInput").ap()
    out = nc.dram_tensor("out", [L, D], F32, kind="ExternalOutput").ap()
    g["x_in"], g["out"] = x_in, out
    g["wq_g"] = nc.dram_tensor("wq_gate", [HID, D], I32, kind="ExternalInput").ap()
    g["s_g"] = nc.dram_tensor("s_gate", [HID], F32, kind="ExternalInput").ap()
    g["wq_u"] = nc.dram_tensor("wq_up", [HID, D], I32, kind="ExternalInput").ap()
    g["s_u"] = nc.dram_tensor("s_up", [HID], F32, kind="ExternalInput").ap()
    g["wq_d"] = nc.dram_tensor("wq_down", [D, HID], I32, kind="ExternalInput").ap()
    g["s_d"] = nc.dram_tensor("s_down", [D], F32, kind="ExternalInput").ap()
    if "A" in phases:
        g["a_wmem"] = nc.dram_tensor("a_wmem", [D, ZC], F32, kind="ExternalInput").ap()
        g["a_kcol"] = nc.dram_tensor("a_kcol", [4, ZC], F32, kind="ExternalInput").ap()
        g["a_wq"] = nc.dram_tensor("a_wq", [D, QD], F32, kind="ExternalInput").ap()
        g["a_theta"] = nc.dram_tensor("a_theta", [128, MEM * M_], F32, kind="ExternalInput").ap()
        g["a_slopes"] = nc.dram_tensor("a_slopes", [K_, 1], F32, kind="ExternalInput").ap()
        g["a_sscale"] = nc.dram_tensor("a_sscale", [K_, 1], F32, kind="ExternalInput").ap()
        g["a_wblk"] = nc.dram_tensor("a_wblk", [2, 2, 128, 1536], F32, kind="ExternalInput").ap()
        g["a_norm"] = nc.dram_tensor("a_norm", [128, 2], F32, kind="ExternalInput").ap()
        g["a_wout"] = nc.dram_tensor("a_wout", [768, D], F32, kind="ExternalInput").ap()
        g["a_ramp"] = nc.dram_tensor("a_ramp", [K_, L], F32, kind="ExternalInput").ap()

    g["wgu_s"] = nc.dram_tensor("wgu_s", [NHB, 128, 2, NDB, 128], BF16, kind="Internal").ap()
    g["wd_s"] = nc.dram_tensor("wd_s", [NDB, 128, NHB, 128], BF16, kind="Internal").ap()
    g["xmid"] = nc.dram_tensor("xmid", [L, D], F32, kind="Internal").ap()
    g["cr_scr"] = nc.dram_tensor("cr_scr", [1, 4128], F32, kind="Internal").ap()
    g["dbg"] = {}
    _dbg_shapes = {"d_col": ([128, 8], F32), "d_h1": ([128, D], BF16),
                   "d_h1T": ([128, NDB * CH], BF16), "d_tmpT": ([128, 2 * CH], BF16),
                   "d_dn": ([128, NDB * CH], BF16), "d_wgu0": ([128, 2 * NDB * 128], BF16),
                   "d_wd0": ([128, NHB * 128], BF16)}
    for name in dbg:
        shp, dt = _dbg_shapes[name]
        g["dbg"][name] = nc.dram_tensor(name, shp, dt, kind="ExternalOutput").ap()

    g["ident_bf"] = nc.inline_tensor(np.eye(128, dtype=ml_dtypes.bfloat16), "ident_bf").ap()
    g["ident_f"] = nc.inline_tensor(np.eye(128, dtype=np.float32), "ident_f").ap()
    g["tri_bf_d"] = nc.inline_tensor(
        np.triu(np.ones((128, 128))).astype(ml_dtypes.bfloat16), "tri_bf").ap()
    g["ones_row_d"] = nc.inline_tensor(np.ones((1, 128), dtype=np.float32), "ones_row").ap()

    with contextlib.ExitStack() as ctx:
        cvals = sorted(set(float(v) for v in CONST_VALUES))
        cpool = ctx.enter_context(nc.sbuf_tensor("constpool", [128, len(cvals)], F32))
        for i, v in enumerate(cvals):
            nc.const_aps.register(F32, v, cpool[:, i:i + 1])
        g["cpool"], g["cvals"] = cpool, cvals

        sb_id = ctx.enter_context(nc.sbuf_tensor("sb_id", [128, 128], BF16))
        sb_idf = ctx.enter_context(nc.sbuf_tensor("sb_idf", [128, 128], F32))
        sb_tri = ctx.enter_context(nc.sbuf_tensor("sb_tri", [128, 128], BF16))
        sb_ones = ctx.enter_context(nc.sbuf_tensor("sb_ones", [1, 128], F32))
        sb_sg = ctx.enter_context(nc.sbuf_tensor("sb_sg", [128, NHB], F32))
        sb_su = ctx.enter_context(nc.sbuf_tensor("sb_su", [128, NHB], F32))
        sb_sd = ctx.enter_context(nc.sbuf_tensor("sb_sd", [128, NDB], F32))
        sb_sdh = ctx.enter_context(nc.sbuf_tensor("sb_sdh", [128, NDB], F32))
        g.update(sb_id=sb_id, sb_idf=sb_idf, sb_tri=sb_tri, sb_ones=sb_ones,
                 sb_sg=sb_sg, sb_su=sb_su, sb_sd=sb_sd, sb_sdh=sb_sdh)

        sem_names = ["c_ld", "t_dq", "t_tr", "t_ev", "t_ld_2", "a_sld", "vch", "a_sch"]
        if dbg:
            sem_names.append("dbg")
        dma_pairs = ["t_ld", "t_wb"]
        sem_names += [f"f_{s}" for s in (
            "sq", "ln1", "ln2", "h1", "htr", "hev",
            "mmg", "evg", "mmu", "evu", "mmd", "evd", "tr2", "xnew")]
        for p in ("f1", "f2"):
            dma_pairs += [f"{p}_{s}" for s in ("wld", "wdld", "x2", "xst")]
            sem_names += [f"{p}_xld_{k}" for k in range(4)]
        sem_names += [f"a_{s}" for s in (
            "sq", "ln1", "ln2", "h1", "htr", "hev", "mmz", "evz",
            "ztr", "zev", "cv", "pw", "pev", "ktr", "kev",
            "mg", "cs", "csev", "dot", "otr", "oev",
            "ymm", "ysg", "ygt", "omm", "xnew")]
        dma_pairs += ["a_wld", "a_x2", "a_xst", "a_cr"]
        sem_names += [f"a_xld_{k}" for k in range(4)]
        for s in dma_pairs:
            sem_names += [s + "_0", s + "_1"]
        sems = {}
        for s in sem_names:
            sems[s] = Ctr(ctx.enter_context(nc.semaphore(s)))
        for s in dma_pairs:
            sems[s] = DmaPair(sems[s + "_0"], sems[s + "_1"])
        g["sems"] = sems

        with nc.Block() as block:
            @block.gpsimd
            def _(gp):
                c = sems["c_ld"]
                c.inc(gp.dma_start(sb_id[:], g["ident_bf"][:]), 16)
                c.inc(gp.dma_start(sb_idf[:], g["ident_f"][:]), 16)
                c.inc(gp.dma_start(sb_tri[:], g["tri_bf_d"][:]), 16)
                c.inc(gp.dma_start(sb_ones[:], g["ones_row_d"][:]), 16)
                c.inc(gp.dma_start(sb_sg[:], g["s_g"].rearrange("(a p) -> p a", p=128),
                                   allow_slow_non_contiguous=True), 16)
                c.inc(gp.dma_start(sb_su[:], g["s_u"].rearrange("(a p) -> p a", p=128),
                                   allow_slow_non_contiguous=True), 16)
                c.inc(gp.dma_start(sb_sd[:], g["s_d"].rearrange("(a p) -> p a", p=128),
                                   allow_slow_non_contiguous=True), 16)

            @block.vector
            def _(v):
                for i, val in enumerate(cvals):
                    v.memset(cpool[:, i:i + 1], val)
                v.wait_ge(sems["c_ld"].h, sems["c_ld"].n)
                v.tensor_scalar_mul(sb_sdh[:], sb_sd[:], 0.5)

        if "T" in phases:
            emit_transcode(nc, g)
        if "F1" in phases:
            dst = g["xmid"] if ("A" in phases or "F2" in phases) else out
            emit_ffn(nc, g, "f1", x_in, dst)
        if "A" in phases:
            src = g["xmid"] if "F1" in phases else x_in
            dst = g["xmid"] if "F2" in phases else out
            emit_attn(nc, g, src, dst)
        if "F2" in phases:
            src = g["xmid"] if ("F1" in phases or "A" in phases) else x_in
            emit_ffn(nc, g, "f2", src, out)

    return nc


def emit_transcode(nc, g):
    sems = g["sems"]
    t_wb = sems["t_wb"]
    t_ld3 = [sems["t_ld_0"], sems["t_ld_1"], sems["t_ld_2"]]
    t_dq, t_tr, t_ev = sems["t_dq"], sems["t_tr"], sems["t_ev"]
    NJ2 = 3 * (NHB // 2)
    with contextlib.ExitStack() as ctx:
        w_i32 = [ctx.enter_context(nc.sbuf_tensor(f"t_wi{i}", [128, 2 * D], I32)) for i in range(3)]
        w_bf = [ctx.enter_context(nc.sbuf_tensor(f"t_wb{i}", [128, 2 * D], BF16)) for i in range(2)]
        stage = [ctx.enter_context(nc.sbuf_tensor(f"t_st{i}", [128, 2 * D], BF16)) for i in range(2)]
        ps_tr = ctx.enter_context(nc.psum_tensor("t_ps", [128, 2, D], BF16))

        def job_info(job):
            mat, k = divmod(job, NHB // 2)
            return mat, 2 * k

        with nc.Block() as block:
            @block.gpsimd
            def _(gp):
                def load(job):
                    mat, hb = job_info(job)
                    buf = job % 3
                    if job >= 3:
                        gp.wait_ge(t_dq.h, job - 2)
                    if mat == 0:
                        srcl = g["wq_g"][hb * 128:(hb + 2) * 128, :] \
                            .rearrange("(h p) c -> p h c", p=128)
                    elif mat == 1:
                        srcl = g["wq_u"][hb * 128:(hb + 2) * 128, :] \
                            .rearrange("(h p) c -> p h c", p=128)
                    else:
                        srcl = g["wq_d"][:, hb * 128:(hb + 2) * 128] \
                            .rearrange("(a p) c -> p a c", p=128)
                    t_ld3[job % 3].inc(gp.dma_start(w_i32[buf][:], srcl), 16)
                def wb(job):
                    mat, hb = job_info(job)
                    buf = job % 2
                    gp.wait_ge(t_ev.h, 2 * (job + 1))
                    for half in range(2):
                        srcv = stage[buf][:, half * D:(half + 1) * D] \
                            .rearrange("p (a c) -> p a c", c=128)
                        if mat < 2:
                            dstv = g["wgu_s"][hb + half, :, mat, :, :]
                        else:
                            dstv = g["wd_s"][:, :, hb + half, :].transpose([1, 0, 2])
                        ins = gp.dma_start(dstv, srcv)
                        t_wb.c[job % 2].inc(ins, 16)
                load(0)
                load(1)
                load(2)
                for job in range(NJ2):
                    if job + 3 < NJ2:
                        load(job + 3)
                    wb(job)
                t_wb.wait_all(gp)

            @block.scalar
            def _(sc):
                for job in range(NJ2):
                    sc.wait_ge(t_ld3[job % 3].h, 16 * (job // 3 + 1))
                    buf3 = job % 3
                    if job >= 2:
                        sc.wait_ge(t_tr.h, 2 * (job - 1))
                    ins = sc.activation(out=w_bf[job % 2][:], in_=w_i32[buf3][:], func=AF.Identity,
                                        bias=-128.0, scale=1.0)
                    t_dq.inc(ins, 1)

            @block.tensor
            def _(t):
                for job in range(NJ2):
                    mat, hb = job_info(job)
                    t.wait_ge(t_dq.h, job + 1)
                    buf = job % 2
                    for half in range(2):
                        if 2 * job + half >= 2:
                            t.wait_ge(t_ev.h, 2 * job + half - 1)
                        for db in range(NDB):
                            if mat < 2:
                                src = w_bf[buf][:, half * D + db * 128:half * D + (db + 1) * 128]
                            else:
                                src = w_bf[buf][:, db * 256 + half * 128:db * 256 + (half + 1) * 128]
                            ins = t.transpose(out=ps_tr[:, half, db * 128:(db + 1) * 128],
                                              in_=src, identity=g["sb_id"][:])
                        t_tr.inc(ins, 1)

            @block.vector
            def _(v):
                for job in range(NJ2):
                    buf = job % 2
                    for half in range(2):
                        v.wait_ge(t_tr.h, 2 * job + half + 1)
                        if job >= 2:
                            jj = job - 2
                            v.wait_ge(t_wb.c[jj % 2].h, 32 * (jj // 2 + 1))
                        v.tensor_copy(stage[buf][:, half * D:half * D + 1024],
                                      ps_tr[:, half, 0:1024])
                        ins = v.tensor_copy(stage[buf][:, half * D + 1024:half * D + 2048],
                                            ps_tr[:, half, 1024:2048])
                        t_ev.inc(ins, 1)


def emit_ffn(nc, g, p, src_x, dst_x):
    sems = g["sems"]
    _shared = {"sq", "ln1", "ln2", "h1", "htr", "hev", "mmg", "evg", "mmu", "evu",
               "mmd", "evd", "tr2", "xnew"}
    S = lambda s: sems[f"f_{s}"] if s in _shared else sems[f"{p}_{s}"]
    PH = 0 if p == "f1" else 1
    J0 = PH * NTT
    I0 = PH * NCH * NHB
    D0 = PH * NCH * NDB
    xldq = [sems[f"{p}_xld_{k}"] for k in range(4)]

    def xld_wait(eng, j):
        eng.wait_ge(xldq[j % 4].h, 16 * (j // 4 + 1))
    with contextlib.ExitStack() as ctx:
        def sqt(j):
            return x_t[(j - 1) % 4] if j >= 1 else xn[0]
        x_t = [ctx.enter_context(nc.sbuf_tensor(f"{p}_x{i}", [128, D], F32)) for i in range(4)]
        col = ctx.enter_context(nc.sbuf_tensor(f"{p}_col", [128, 8], F32))
        h1 = [ctx.enter_context(nc.sbuf_tensor(f"{p}_h1{i}", [128, D], BF16)) for i in range(4)]
        h1T = [ctx.enter_context(nc.sbuf_tensor(f"{p}_h1T{i}", [128, NDB, CH], BF16)) for i in range(2)]
        wgu = [ctx.enter_context(nc.sbuf_tensor(f"{p}_wgu{i}", [128, 2, NDB, 128], BF16)) for i in range(2)]
        sgb = [ctx.enter_context(nc.sbuf_tensor(f"{p}_sg{i}", [128, CH], BF16)) for i in range(2)]
        tmpT = ctx.enter_context(nc.sbuf_tensor(f"{p}_tmpT", [128, NHB, CH], BF16))
        wd = [ctx.enter_context(nc.sbuf_tensor(f"{p}_wd{i}", [128, NHB, 128], BF16)) for i in range(2)]
        dn = ctx.enter_context(nc.sbuf_tensor(f"{p}_dn", [128, NDB, CH], BF16))
        xo = [ctx.enter_context(nc.sbuf_tensor(f"{p}_xo0", [128, D], F32))]
        xn = [ctx.enter_context(nc.sbuf_tensor(f"{p}_xn{i}", [128, D], F32)) for i in range(2)]
        ps_gu = ctx.enter_context(nc.psum_tensor(f"{p}_psgu", [128, 4, 512], F32))
        ps_tr = ctx.enter_context(nc.psum_tensor(f"{p}_pstr", [128, 2, D], BF16))

        with nc.Block() as block:
            @block.gpsimd
            def _(gp):
                dbg = g["dbg"]
                def xload1(j):
                    if j >= 4:
                        gp.wait_ge(S("ln2").h, 2 * (J0 + j - 3))
                        gp.wait_ge(S("ln1").h, J0 + j - 2)
                    r0 = j * 128
                    xldq[j % 4].inc(gp.dma_start(x_t[j % 4][:], src_x[r0:r0 + 128, :]), 16)
                def xloads(ch):
                    for tt in range(TCH):
                        xload1(ch * TCH + tt)
                def wguloads(ch):
                    if ch >= 1:
                        for tt in range(TCH):
                            xopair(ch - 1, tt)
                    for hb in range(NHB):
                        i = ch * NHB + hb
                        if i >= 2:
                            gp.wait_ge(S("mmu").h, I0 + i - 1)
                        S("wld").inc(i, gp.dma_start(wgu[i % 2][:], g["wgu_s"][hb, :, :, :, :]))
                        if hb % 11 == 10 and ch + 1 < NCH:
                            xload1((ch + 1) * TCH + hb // 11)
                def wdloads(ch):
                    for db in range(NDB):
                        i = ch * NDB + db
                        if i >= 2:
                            gp.wait_ge(S("mmd").h, D0 + i - 1)
                        S("wdld").inc(i, gp.dma_start(wd[i % 2][:], g["wd_s"][db, :, :, :]))
                def xopair(ch, tt):
                    j = ch * TCH + tt
                    if j >= 1:
                        gp.wait_ge(S("xnew").h, J0 + j)
                    r0 = j * 128
                    S("x2").inc(j, gp.dma_start(xo[0][:], src_x[r0:r0 + 128, :]))
                    gp.wait_ge(S("xnew").h, J0 + j + 1)
                    S("xst").inc(j, gp.dma_start(dst_x[r0:r0 + 128, :], xn[j % 2][:]))
                xloads(0)
                wguloads(0)
                wdloads(0)
                for ch in range(1, NCH):
                    wguloads(ch)
                    wdloads(ch)
                for tt in range(TCH):
                    xopair(NCH - 1, tt)
                S("xst").wait_all(gp)
            @block.vector
            def _(v):
                vch = sems["vch"]
                def VC(ins):
                    n = vch.inc(ins, 1)
                    v.wait_ge(vch.h, n)
                def ln_evac(j):
                    ch = j // TCH
                    v.wait_ge(S("htr").h, J0 + j + 1)
                    if ch >= 2:
                        v.wait_ge(S("mmu").h, I0 + (ch - 1) * NHB)
                    for half in range(2):
                        srcv = ps_tr[:, j % 2, half * 1024:(half + 1) * 1024] \
                            .rearrange("p (a c) -> p a c", c=128)
                        dstv = h1T[ch % 2][:, half * 8:(half + 1) * 8,
                                           (j % TCH) * 128:(j % TCH) * 128 + 128]
                        ins = v.tensor_copy(dstv, srcv)
                    S("hev").inc(ins, 1)
                def ln_tile(j):
                    b = j % 4
                    ch = j // TCH
                    xld_wait(v, j)
                    VC(v.tensor_reduce(col[:, 0:1], x_t[b][:], axis=AX.X, op=ALU.add))
                    VC(v.tensor_scalar_mul(col[:, 1:2], col[:, 0:1], 1.0 / D))
                    v.wait_ge(S("sq").h, J0 + j + 1)
                    VC(v.tensor_reduce(col[:, 2:3], sqt(j)[:], axis=AX.X, op=ALU.add))
                    VC(v.tensor_tensor(col[:, 4:5], col[:, 1:2], col[:, 1:2], op=ALU.mult))
                    ins = v.tensor_scalar(out=col[:, 3:4], in0=col[:, 2:3], scalar1=1.0 / D,
                                          scalar2=col[:, 4:5], op0=ALU.mult, op1=ALU.subtract)
                    S("ln1").inc(ins, 1)
                    v.wait_ge(S("ln2").h, 2 * (J0 + j) + 1)
                    VC(v.reciprocal(col[:, 5:6], col[:, 4:5]))
                    ins = v.scalar_tensor_tensor(out=col[:, 6:7], in0=col[:, 1:2], scalar=-1.0,
                                                 in1=col[:, 5:6], op0=ALU.mult, op1=ALU.mult)
                    S("h1").inc(ins, 1)
                for tt in range(TCH):
                    ln_tile(tt)
                    ln_evac(tt)
                for ch in range(NCH):
                    for hb in range(NHB):
                        i = ch * NHB + hb
                        bank_u = (hb % 2) * 2 + 1
                        v.wait_ge(S("mmu").h, I0 + i + 1)
                        v.wait_ge(S("evg").h, I0 + i + 1)
                        ins = v.scalar_tensor_tensor(
                            out=tmpT[:, hb, :], in0=ps_gu[:, bank_u, :],
                            scalar=g["sb_su"][:, hb:hb + 1], in1=sgb[hb % 2][:],
                            op0=ALU.mult, op1=ALU.mult)
                        S("evu").inc(ins, 1)
                        if ch + 1 < NCH and hb % 11 == 10:
                            ln_tile((ch + 1) * TCH + hb // 11)
                    for db in range(NDB):
                        i = ch * NDB + db
                        v.wait_ge(S("mmd").h, D0 + i + 1)
                        ins = v.tensor_scalar_mul(dn[:, db, :], ps_gu[:, db % 4, :],
                                                  g["sb_sdh"][:, db:db + 1])
                        S("evd").inc(ins, 1)
                    for tt in range(TCH):
                        j = ch * TCH + tt
                        b = j % 2
                        v.wait_ge(S("tr2").h, J0 + j + 1)
                        S("x2").wait(v, j)
                        if j >= 2:
                            S("xst").wait(v, j - 2)
                        ins = v.tensor_tensor(out=xn[b][:], in0=ps_tr[:, b, :],
                                              in1=xo[0][:], op=ALU.add)
                        S("xnew").inc(ins, 1)
                    if ch + 1 < NCH:
                        for tt in range(TCH):
                            ln_evac((ch + 1) * TCH + tt)

            @block.scalar
            def _(sc):
                def ln_tile_s(j):
                    b4 = j % 4
                    xld_wait(sc, j)
                    if j >= 1:
                        sc.wait_ge(S("ln1").h, J0 + j)
                    ins = sc.activation(out=sqt(j)[:], in_=x_t[b4][:], func=AF.Square)
                    S("sq").inc(ins, 1)
                    sc.wait_ge(S("ln1").h, J0 + j + 1)
                    ins = sc.activation(out=col[:, 4:5], in_=col[:, 3:4], func=AF.Sqrt,
                                        bias=EPS_LN)
                    S("ln2").inc(ins, 1)
                    sc.wait_ge(S("h1").h, J0 + j + 1)
                    if j >= 4:
                        sc.wait_ge(S("htr").h, J0 + j - 3)
                    ins = sc.activation(out=h1[b4][:], in_=x_t[b4][:], func=AF.Identity,
                                        scale=col[:, 5:6], bias=col[:, 6:7])
                    S("ln2").inc(ins, 1)
                for tt in range(TCH):
                    ln_tile_s(tt)
                for ch in range(NCH):
                    for hb in range(NHB):
                        i = ch * NHB + hb
                        bank_g = (hb % 2) * 2
                        sc.wait_ge(S("mmg").h, I0 + i + 1)
                        if i >= 2:
                            sc.wait_ge(S("evu").h, I0 + i - 1)
                        ins = sc.activation(out=sgb[hb % 2][:], in_=ps_gu[:, bank_g, :],
                                            func=AF.Silu, scale=g["sb_sg"][:, hb:hb + 1])
                        S("evg").inc(ins, 1)
                        if ch + 1 < NCH and hb % 11 == 10:
                            ln_tile_s((ch + 1) * TCH + hb // 11)
            @block.tensor
            def _(t):
                for ch in range(NCH):
                    for tt in range(TCH):
                        j = ch * TCH + tt
                        b = j % 2
                        t.wait_ge(S("ln2").h, 2 * (J0 + j + 1))
                        if j >= 2:
                            t.wait_ge(S("hev").h, J0 + j - 1)
                        if ch >= 1:
                            t.wait_ge(S("xnew").h, J0 + (ch - 1) * TCH + TCH - 2 + (tt % 2) + 1)
                        for db in range(NDB):
                            ins = t.transpose(out=ps_tr[:, b, db * 128:(db + 1) * 128],
                                              in_=h1[j % 4][:, db * 128:(db + 1) * 128],
                                              identity=g["sb_id"][:])
                        S("htr").inc(ins, 1)
                    for hb in range(NHB):
                        i = ch * NHB + hb
                        bank_g = (hb % 2) * 2
                        bank_u = bank_g + 1
                        S("wld").wait(t, i)
                        if hb == 0:
                            t.wait_ge(S("hev").h, J0 + (ch + 1) * TCH)
                            t.wait_ge(S("evd").h, D0 + ch * NDB)
                        if i >= 2:
                            t.wait_ge(S("evg").h, I0 + i - 1)
                            t.wait_ge(S("evu").h, I0 + i - 1)
                        for db in range(NDB):
                            ins = t.matmul(ps_gu[:, bank_g, :], wgu[i % 2][:, 0, db, :],
                                           h1T[ch % 2][:, db, :], start=(db == 0),
                                           stop=(db == NDB - 1))
                        S("mmg").inc(ins, 1)
                        for db in range(NDB):
                            ins = t.matmul(ps_gu[:, bank_u, :], wgu[i % 2][:, 1, db, :],
                                           h1T[ch % 2][:, db, :], start=(db == 0),
                                           stop=(db == NDB - 1))
                        S("mmu").inc(ins, 1)
                    for db in range(NDB):
                        i = ch * NDB + db
                        S("wdld").wait(t, i)
                        if db == 0:
                            t.wait_ge(S("evu").h, I0 + (ch + 1) * NHB)
                            t.wait_ge(S("evg").h, I0 + (ch + 1) * NHB)
                        if i >= 4:
                            t.wait_ge(S("evd").h, D0 + i - 3)
                        for hb in range(NHB):
                            ins = t.matmul(ps_gu[:, db % 4, :], wd[i % 2][:, hb, :],
                                           tmpT[:, hb, :], start=(hb == 0), stop=(hb == NHB - 1))
                        S("mmd").inc(ins, 1)
                    for tt in range(TCH):
                        j = ch * TCH + tt
                        b = j % 2
                        t.wait_ge(S("evd").h, D0 + (ch + 1) * NDB)
                        if j >= 2:
                            t.wait_ge(S("xnew").h, J0 + j - 1)
                        for db in range(NDB):
                            ins = t.transpose(out=ps_tr[:, b, db * 128:(db + 1) * 128],
                                              in_=dn[:, db, tt * 128:(tt + 1) * 128],
                                              identity=g["sb_id"][:])
                        S("tr2").inc(ins, 1)


def emit_attn(nc, g, src_x, dst_x):
    sems = g["sems"]
    A = lambda s: sems[f"a_{s}"]
    axldq = [sems[f"a_xld_{k}"] for k in range(4)]

    def axld_wait(eng, j):
        eng.wait_ge(axldq[j % 4].h, 16 * (j // 4 + 1))
    NC9 = 9
    CW = [512] * 8 + [32]
    CO = [512 * i for i in range(9)]

    with contextlib.ExitStack() as actx:
        w_mem_bf = actx.enter_context(nc.sbuf_tensor("a_wmemb", [128, NDB, ZC], BF16))
        w_q_bf = actx.enter_context(nc.sbuf_tensor("a_wqb", [128, NDB, QD], BF16))
        wblk_bf = actx.enter_context(nc.sbuf_tensor("a_wblkb", [128, 2, 2, 1536], BF16))
        theta = actx.enter_context(nc.sbuf_tensor("a_theta_sb", [128, MEM * M_], F32))
        norm_sb = actx.enter_context(nc.sbuf_tensor("a_norm_sb", [128, 2], F32))
        acol = actx.enter_context(nc.sbuf_tensor("a_acol", [32, 8], F32))
        kc_sb = actx.enter_context(nc.sbuf_tensor("a_kc", [128, 12], F32))
        oT = [actx.enter_context(nc.sbuf_tensor(f"a_oT{i}", [128, 2, L], BF16)) for i in range(2)]

        with nc.Block() as block:
            @block.gpsimd
            def _(gp):
                c = A("wld")
                c.inc(0, gp.dma_start(
                    w_mem_bf[:], g["a_wmem"].rearrange("(a p) c -> p a c", p=128)))
                c.inc(1, gp.dma_start(
                    w_q_bf[:], g["a_wq"].rearrange("(a p) c -> p a c", p=128)))
                c.inc(2, gp.dma_start(theta[:], g["a_theta"][:]))
                c.inc(3, gp.dma_start(norm_sb[:], g["a_norm"][:]))
                sems["a_sld"].inc(gp.dma_start(acol[:, 0:1], g["a_slopes"][:]), 16)
                sems["a_sld"].inc(gp.dma_start(acol[:, 1:2], g["a_sscale"][:]), 16)
                for ct in range(3):
                    w = 128 if ct < 2 else 32
                    ins = gp.dma_start(kc_sb[0:w, ct * 4:(ct + 1) * 4],
                                       g["a_kcol"][:, ct * 128:ct * 128 + w].transpose([1, 0]),
                                       allow_slow_non_contiguous=True)
                    if ct < 2:
                        sems["a_sld"].inc(ins, 16)
                c.inc(6, ins)
                c.inc(7, gp.dma_start(wblk_bf[:], g["a_wblk"].transpose([2, 0, 1, 3])))
                gp.wait_ge(c.c[0].h, c.c[0].n)
                gp.wait_ge(c.c[1].h, c.c[1].n)
                gp.wait_ge(sems["a_sld"].h, sems["a_sld"].n)

            @block.vector
            def _(v):
                vch = sems["vch"]
                def VC(ins):
                    n = vch.inc(ins, 1)
                    v.wait_ge(vch.h, n)
                v.wait_ge(A("wld").c[0].h, A("wld").c[0].n)
                v.wait_ge(A("wld").c[1].h, A("wld").c[1].n)
                v.wait_ge(sems["a_sld"].h, sems["a_sld"].n)
                for ri in range(2):
                    for kt in range(2):
                        VC(v.tensor_scalar_mul(wblk_bf[:, ri, kt, :], wblk_bf[:, ri, kt, :],
                                               norm_sb[:, kt:kt + 1]))
                v.wait_ge(A("sq").h, 1)
                VC(v.tensor_scalar_max(acol[0:32, 5:6], acol[0:32, 0:1], 0.0))
                VC(v.tensor_add(acol[0:32, 6:7], acol[0:32, 5:6], acol[0:32, 4:5]))
                ins = v.tensor_scalar_mul(acol[0:32, 7:8], acol[0:32, 6:7], -1.0)
                A("cv").inc(ins, 1)

            @block.scalar
            def _(sc):
                sch = sems["a_sch"]
                def SC(ins):
                    n = sch.inc(ins, 1)
                    sc.wait_ge(sch.h, n)
                sc.wait_ge(sems["a_sld"].h, 64)
                SC(sc.activation(out=acol[0:32, 2:3], in_=acol[0:32, 0:1], func=AF.Abs))
                SC(sc.activation(out=acol[0:32, 3:4], in_=acol[0:32, 2:3], func=AF.Exp,
                                 scale=-1.0))
                ins = sc.activation(out=acol[0:32, 4:5], in_=acol[0:32, 3:4], func=AF.Ln,
                                    bias=1.0)
                A("sq").inc(ins, 1)

        with contextlib.ExitStack() as zctx:
            k_val = zctx.enter_context(nc.sbuf_tensor("a_kval", [128, NTT, MEM], F32))
            p_w = zctx.enter_context(nc.sbuf_tensor("a_pw", [128, NTT, K_], F32))
            q_sb = zctx.enter_context(nc.sbuf_tensor("a_qsb", [128, NTT, QD], BF16))
            zscope = contextlib.ExitStack()
            z_sb = zscope.enter_context(nc.sbuf_tensor("a_zsb", [128, NTT, ZC], F32))

            with contextlib.ExitStack() as ctx:
                x_t = [ctx.enter_context(nc.sbuf_tensor(f"a_x{i}", [128, D], F32)) for i in range(4)]
                sq_dump = [ctx.enter_context(nc.sbuf_tensor(f"a_sqd{i}", [128, D], F32)) for i in range(2)]
                col = ctx.enter_context(nc.sbuf_tensor("a_col", [128, 2, 8], F32))
                h1 = [ctx.enter_context(nc.sbuf_tensor(f"a_h1{i}", [128, D], BF16)) for i in range(4)]
                h2T = ctx.enter_context(nc.sbuf_tensor("a_h2T", [128, NDB, 128], BF16))
                ps_tr = ctx.enter_context(nc.psum_tensor("a_pstr", [128, 2, D], BF16))
                ps_z = ctx.enter_context(nc.psum_tensor("a_psz", [128, 2, 512], F32))
                ps_q = ctx.enter_context(nc.psum_tensor("a_psq", [128, 2, 512], F32))

                with nc.Block() as block:
                    @block.gpsimd
                    def _(gp):
                        for j in range(NTT):
                            if j >= 4:
                                gp.wait_ge(A("ln2").h, 2 * (j - 3))
                                gp.wait_ge(A("ln1").h, j - 2)
                            axldq[j % 4].inc(gp.dma_start(
                                x_t[j % 4][:], src_x[j * 128:(j + 1) * 128, :]), 16)
                        gp.wait_ge(A("evz").h, NTT)

                    @block.vector
                    def _(v):
                        vch = sems["vch"]
                        def VC(ins):
                            n = vch.inc(ins, 1)
                            v.wait_ge(vch.h, n)
                        for j in range(NTT):
                            b = j % 2
                            cl = col[:, j % 2, :]
                            axld_wait(v, j)
                            if j >= 2:
                                v.wait_ge(A("ln2").h, 2 * (j - 1))
                            VC(v.tensor_reduce(cl[:, 0:1], x_t[j % 4][:], axis=AX.X, op=ALU.add))
                            VC(v.tensor_scalar_mul(cl[:, 1:2], cl[:, 0:1], 1.0 / D))
                            v.wait_ge(A("sq").h, 1 + j + 1)
                            VC(v.tensor_reduce(cl[:, 2:3], sq_dump[j % 2][:], axis=AX.X, op=ALU.add))
                            VC(v.tensor_tensor(cl[:, 4:5], cl[:, 1:2], cl[:, 1:2], op=ALU.mult))
                            ins = v.tensor_scalar(out=cl[:, 3:4], in0=cl[:, 2:3], scalar1=1.0 / D,
                                                  scalar2=cl[:, 4:5], op0=ALU.mult, op1=ALU.subtract)
                            A("ln1").inc(ins, 1)
                            v.wait_ge(A("ln2").h, 2 * j + 1)
                            VC(v.reciprocal(cl[:, 5:6], cl[:, 4:5]))
                            ins = v.scalar_tensor_tensor(out=cl[:, 6:7], in0=cl[:, 1:2], scalar=-1.0,
                                                         in1=cl[:, 5:6], op0=ALU.mult, op1=ALU.mult)
                            A("h1").inc(ins, 1)
                            v.wait_ge(A("htr").h, j + 1)
                            for half in range(2):
                                srcv = ps_tr[:, b, half * 1024:(half + 1) * 1024] \
                                    .rearrange("p (a c) -> p a c", c=128)
                                dstv = h2T[:, half * 8:(half + 1) * 8, :]
                                ins = v.tensor_copy(dstv, srcv)
                            A("hev").inc(ins, 1)
                            v.wait_ge(A("mmz").h, 2 * (j + 1))
                            v.tensor_copy(z_sb[:, j, :], ps_z[:, b, 0:ZC])
                            ins = v.tensor_copy(q_sb[:, j, :], ps_q[:, b, :])
                            A("evz").inc(ins, 1)

                    @block.scalar
                    def _(sc):
                        def square(j):
                            axld_wait(sc, j)
                            if j >= 2:
                                sc.wait_ge(A("ln1").h, j - 1)
                            ins = sc.activation(out=sq_dump[j % 2][:], in_=x_t[j % 4][:],
                                                func=AF.Square)
                            A("sq").inc(ins, 1)
                        square(0)
                        for j in range(NTT):
                            cl = col[:, j % 2, :]
                            sc.wait_ge(A("ln1").h, j + 1)
                            ins = sc.activation(out=cl[:, 4:5], in_=cl[:, 3:4], func=AF.Sqrt,
                                                bias=EPS_LN)
                            A("ln2").inc(ins, 1)
                            if j + 1 < NTT:
                                square(j + 1)
                            sc.wait_ge(A("h1").h, j + 1)
                            if j >= 4:
                                sc.wait_ge(A("htr").h, j - 3)
                            ins = sc.activation(out=h1[j % 4][:], in_=x_t[j % 4][:], func=AF.Identity,
                                                scale=cl[:, 5:6], bias=cl[:, 6:7])
                            A("ln2").inc(ins, 1)

                    @block.tensor
                    def _(t):
                        for j in range(NTT):
                            b = j % 2
                            t.wait_ge(A("ln2").h, 2 * (j + 1))
                            if j >= 2:
                                t.wait_ge(A("hev").h, j - 1)
                            for db in range(NDB):
                                ins = t.transpose(out=ps_tr[:, b, db * 128:(db + 1) * 128],
                                                  in_=h1[j % 4][:, db * 128:(db + 1) * 128],
                                                  identity=g["sb_id"][:])
                            A("htr").inc(ins, 1)
                            t.wait_ge(A("hev").h, j + 1)
                            if j >= 2:
                                t.wait_ge(A("evz").h, j - 1)
                            for db in range(NDB):
                                ins = t.matmul(ps_z[:, b, 0:ZC], h2T[:, db, :],
                                               w_mem_bf[:, db, :], start=(db == 0),
                                               stop=(db == NDB - 1))
                            A("mmz").inc(ins, 1)
                            for db in range(NDB):
                                ins = t.matmul(ps_q[:, b, :], h2T[:, db, :],
                                               w_q_bf[:, db, :], start=(db == 0),
                                               stop=(db == NDB - 1))
                            A("mmz").inc(ins, 1)

            with contextlib.ExitStack() as ctx:
                zT = ctx.enter_context(nc.sbuf_tensor("a_zT", [128, 3, 2052], F32))
                zc = ctx.enter_context(nc.sbuf_tensor("a_zc", [128, 3, L], F32))
                ramp = ctx.enter_context(nc.sbuf_tensor("a_ramp_sb", [32, L], F32))
                pwT = ctx.enter_context(nc.sbuf_tensor("a_pwT", [32, L], F32))
                ps_f = ctx.enter_context(nc.psum_tensor("a_psf", [128, 2, D], F32))

                with nc.Block() as block:
                    @block.gpsimd
                    def _(gp):
                        nr = sems["a_sld"].inc(gp.dma_start(ramp[:], g["a_ramp"][:]), 16)
                        gp.wait_ge(A("kev").h, 2 * NTT)

                    @block.vector
                    def _(v):
                        vch = sems["vch"]
                        def VC(ins):
                            n = vch.inc(ins, 1)
                            v.wait_ge(vch.h, n)
                        for ct in range(3):
                            v.memset(zT[:, ct, 0:4], 0.0)
                        ins = v.memset(pwT[0:1, 0:1], 0.0)
                        A("ztr").inc(ins, 1)
                        for ct in range(3):
                            v.wait_ge(A("ztr").h, ct + 2)
                            w = 128 if ct < 2 else 32
                            ins = v.tensor_copy(zT[0:w, ct, 4:2052], ps_f[0:w, ct % 2, :])
                            A("zev").inc(ins, 1)
                        for ct in range(3):
                            w = 128 if ct < 2 else 32
                            v.wait_ge(A("zev").h, ct + 1)
                            v.tensor_scalar_mul(zc[0:w, ct, :], zT[0:w, ct, 4:2052],
                                                kc_sb[0:w, ct * 4 + 3:ct * 4 + 4])
                            for jj in (2, 1, 0):
                                v.scalar_tensor_tensor(
                                    out=zc[0:w, ct, :], in0=zT[0:w, ct, jj + 1:jj + 1 + 2048],
                                    scalar=kc_sb[0:w, ct * 4 + jj:ct * 4 + jj + 1],
                                    in1=zc[0:w, ct, :], op0=ALU.mult, op1=ALU.add)
                            ins = v.tensor_copy(zc[0:1, ct, 0:1], zc[0:1, ct, 0:1])
                            A("cv").inc(ins, 1)
                        v.wait_ge(A("cv").h, 4)
                        v.wait_ge(sems["a_sld"].h, sems["a_sld"].n)
                        VC(v.tensor_scalar(pwT[:], zc[0:32, 2, :], acol[0:32, 1:2], 20.0,
                                           op0=ALU.mult, op1=ALU.min))
                        VC(v.tensor_scalar_max(pwT[:], pwT[:], -20.0))
                        VC(v.tensor_scalar_mul(ramp[:], ramp[:], acol[0:32, 7:8]))
                        ins = v.tensor_tensor(pwT[:], pwT[:], ramp[:], op=ALU.add)
                        A("pw").inc(ins, 1)
                        for j in range(NTT):
                            v.wait_ge(A("ktr").h, j + 1)
                            v.tensor_copy(k_val[:, j, :], ps_f[:, j % 2, 0:MEM])
                            ins = v.tensor_copy(p_w[:, j, :], ps_f[:, j % 2, MEM:MEM + K_])
                            A("kev").inc(ins, 2)

                    @block.scalar
                    def _(sc):
                        sc.wait_ge(A("pw").h, 1)
                        ins = sc.activation(out=pwT[:], in_=pwT[:], func=AF.Exp)
                        A("pw").inc(ins, 1)

                    @block.tensor
                    def _(t):
                        for ct in range(3):
                            w = 128 if ct < 2 else 32
                            if ct >= 2:
                                t.wait_ge(A("zev").h, ct - 1)
                            for j in range(NTT):
                                ins = t.transpose(
                                    out=ps_f[0:w, ct % 2, j * 128:(j + 1) * 128],
                                    in_=z_sb[:, j, ct * 128:ct * 128 + w],
                                    identity=g["sb_idf"][:])
                            A("ztr").inc(ins, 1)
                        t.wait_ge(A("pw").h, 2)
                        t.wait_ge(A("cv").h, 4)
                        for j in range(NTT):
                            if j >= 2:
                                t.wait_ge(A("kev").h, 2 * (j - 1))
                            for ct in range(2):
                                t.transpose(out=ps_f[:, j % 2, ct * 128:(ct + 1) * 128],
                                            in_=zc[:, ct, j * 128:(j + 1) * 128],
                                            identity=g["sb_idf"][:])
                            ins = t.transpose(out=ps_f[:, j % 2, MEM:MEM + K_],
                                              in_=pwT[0:32, j * 128:(j + 1) * 128],
                                              identity=g["sb_idf"][0:32, 0:32])
                            A("ktr").inc(ins, 1)

            zscope.close()
            with contextlib.ExitStack() as ctx:
                phi = ctx.enter_context(nc.sbuf_tensor("a_phi", [128, 2048], F32))
                trg = ctx.enter_context(nc.sbuf_tensor("a_trg", [128, 2048], F32))
                prod = ctx.enter_context(nc.sbuf_tensor("a_prod", [128, 2048], F32))
                prod2 = ctx.enter_context(nc.sbuf_tensor("a_prod2", [128, 2048], F32))
                merged = [ctx.enter_context(nc.sbuf_tensor(f"a_mg{i}", [128, 4128], BF16))
                          for i in range(2)]
                cs = [ctx.enter_context(nc.sbuf_tensor(f"a_cs{i}", [128, 4128], F32))
                      for i in range(2)]
                qf = ctx.enter_context(nc.sbuf_tensor("a_qf", [128, QD], F32))
                ore = ctx.enter_context(nc.sbuf_tensor("a_ore", [128, MEM], F32))
                oim = ctx.enter_context(nc.sbuf_tensor("a_oim", [128, MEM], F32))
                ot1 = ctx.enter_context(nc.sbuf_tensor("a_ot1", [128, 2 * K_], F32))
                obf = ctx.enter_context(nc.sbuf_tensor("a_obf", [128, 2, MEM], BF16))
                carry = ctx.enter_context(nc.sbuf_tensor("a_carry", [128, 4128], F32))
                ps_cs = ctx.enter_context(nc.psum_tensor("a_pscs", [128, 2, 512], F32))
                ps_o = ctx.enter_context(nc.psum_tensor("a_pso", [128, 2, MEM], BF16))

                def qviews(b):
                    vs = []
                    for kq in range(KQ):
                        base = kq * 128
                        qre_v = qf[:, base:base + 127:2].rearrange("p (h m) -> p h m", m=M_) \
                            .unsqueeze(1).broadcast_to((128, 8, 8, M_))
                        qim_v = qf[:, base + 1:base + 128:2].rearrange("p (h m) -> p h m", m=M_) \
                            .unsqueeze(1).broadcast_to((128, 8, 8, M_))
                        csr = cs[b][:, K_ + kq * 512:K_ + (kq + 1) * 512] \
                            .rearrange("p (r h m) -> p r h m", h=8, m=M_)
                        csi = cs[b][:, K_ + 2048 + kq * 512:K_ + 2048 + (kq + 1) * 512] \
                            .rearrange("p (r h m) -> p r h m", h=8, m=M_)
                        pr = prod[:, kq * 512:(kq + 1) * 512].rearrange(
                            "p (r h m) -> p r h m", h=8, m=M_)
                        pr2 = prod2[:, kq * 512:(kq + 1) * 512].rearrange(
                            "p (r h m) -> p r h m", h=8, m=M_)
                        vs.append((qre_v, qim_v, csr, csi, pr, pr2))
                    return vs

                with nc.Block() as block:
                    @block.gpsimd
                    def _(gp):
                        cr0, cr1 = A("cr").c
                        for j in range(NTT - 1):
                            gp.wait_ge(A("csev").h, NC9 * (j + 1))
                            if j >= 1:
                                gp.wait_ge(cr1.h, 16 * j)
                            cr0.inc(gp.dma_start(g["cr_scr"], cs[j % 2][127:128, :]), 16)
                            gp.wait_ge(cr0.h, 16 * (j + 1))
                            cr1.inc(gp.dma_start(
                                carry[:], g["cr_scr"][0:1, :].broadcast_to((128, 4128))), 16)
                        gp.wait_ge(A("oev").h, 2 * NTT)

                    @block.vector
                    def _(v):
                        vch = sems["vch"]
                        def VC(ins):
                            n = vch.inc(ins, 1)
                            v.wait_ge(vch.h, n)
                        def trig_pre(j):
                            kv = k_val[:, j, :].unsqueeze(2).broadcast_to((128, MEM, M_))
                            th3 = theta[:].rearrange("p (a c) -> p a c", c=M_)
                            ph3 = phi[:].rearrange("p (a c) -> p a c", c=M_)
                            v.tensor_tensor(out=ph3, in0=kv, in1=th3, op=ALU.mult)
                            v.tensor_scalar(prod[:], phi[:], INV2PI, MAGIC,
                                            op0=ALU.mult, op1=ALU.add)
                            v.tensor_scalar_sub(prod[:], prod[:], MAGIC)
                            v.scalar_tensor_tensor(out=trg[:], in0=prod[:], scalar=TWOPI,
                                                   in1=phi[:], op0=ALU.mult, op1=ALU.subtract)
                            ins = v.tensor_scalar(trg[:], trg[:], -PI_CLAMP, PI_CLAMP,
                                                  op0=ALU.max, op1=ALU.min)
                            A("mg").inc(ins, 1)
                        def merged_build(j):
                            b = j % 2
                            v.wait_ge(A("pev").h, 2 * j + 2)
                            if j >= 2:
                                v.wait_ge(A("cs").h, NC9 * (j - 1))
                            pw3 = p_w[:, j, :].unsqueeze(2).broadcast_to((128, K_, 64))
                            v.tensor_copy(merged[b][:, 0:K_], p_w[:, j, :])
                            mg3 = merged[b][:, K_:K_ + 2048].rearrange("p (a c) -> p a c", c=64)
                            v.tensor_tensor(out=mg3, in0=phi[:].rearrange("p (a c) -> p a c", c=64),
                                            in1=pw3, op=ALU.mult)
                            mg4 = merged[b][:, K_ + 2048:].rearrange("p (a c) -> p a c", c=64)
                            ins = v.tensor_tensor(out=mg4,
                                                  in0=trg[:].rearrange("p (a c) -> p a c", c=64),
                                                  in1=pw3, op=ALU.mult)
                            A("mg").inc(ins, 1)
                        trig_pre(0)
                        merged_build(0)
                        for j in range(NTT):
                            b = j % 2
                            for c9 in range(NC9):
                                v.wait_ge(A("cs").h, NC9 * j + c9 + 1)
                                if j == 0:
                                    ins = v.tensor_copy(cs[b][:, CO[c9]:CO[c9] + CW[c9]],
                                                        ps_cs[:, c9 % 2, 0:CW[c9]])
                                else:
                                    if c9 == 0:
                                        v.wait_ge(A("cr").c[1].h, 16 * j)
                                    ins = v.tensor_tensor(
                                        out=cs[b][:, CO[c9]:CO[c9] + CW[c9]],
                                        in0=ps_cs[:, c9 % 2, 0:CW[c9]],
                                        in1=carry[:, CO[c9]:CO[c9] + CW[c9]], op=ALU.add)
                                A("csev").inc(ins, 1)
                            if j + 1 < NTT:
                                trig_pre(j + 1)
                            VC(v.tensor_copy(qf[:], q_sb[:, j, :]))
                            vs = qviews(b)
                            for (qre_v, qim_v, csr, csi, pr, pr2) in vs:
                                v.tensor_tensor(out=pr, in0=csr, in1=qre_v, op=ALU.mult)
                                v.tensor_tensor(out=pr2, in0=csi, in1=qim_v, op=ALU.mult)
                            VC(v.tensor_add(prod[:], prod[:], prod2[:]))
                            for kq in range(KQ):
                                pr3 = prod[:, kq * 512:(kq + 1) * 512].rearrange(
                                    "p (a m) -> p a m", m=M_)
                                ins = v.tensor_reduce(ore[:, kq * 64:(kq + 1) * 64], pr3,
                                                      axis=AX.X, op=ALU.add)
                            VC(ins)
                            for (qre_v, qim_v, csr, csi, pr, pr2) in vs:
                                v.tensor_tensor(out=pr, in0=csi, in1=qre_v, op=ALU.mult)
                                v.tensor_tensor(out=pr2, in0=csr, in1=qim_v, op=ALU.mult)
                            VC(v.tensor_sub(prod[:], prod[:], prod2[:]))
                            for kq in range(KQ):
                                pr3 = prod[:, kq * 512:(kq + 1) * 512].rearrange(
                                    "p (a m) -> p a m", m=M_)
                                ins = v.tensor_reduce(oim[:, kq * 64:(kq + 1) * 64], pr3,
                                                      axis=AX.X, op=ALU.add)
                            VC(ins)
                            if j + 1 < NTT:
                                merged_build(j + 1)
                            VC(v.tensor_scalar_max(ot1[:, 0:K_], cs[b][:, 0:K_], 1e-4))
                            VC(v.reciprocal(ot1[:, K_:2 * K_], ot1[:, 0:K_]))
                            idv = ot1[:, K_:2 * K_].unsqueeze(2).broadcast_to((128, K_, H_))
                            for ri, srcb in ((0, ore), (1, oim)):
                                VC(v.tensor_tensor(
                                    out=obf[:, ri, :].rearrange("p (a c) -> p a c", c=H_),
                                    in0=srcb[:].rearrange("p (a c) -> p a c", c=H_),
                                    in1=idv, op=ALU.mult))
                            A("dot").inc(v.tensor_copy(ot1[0:1, 0:1], ot1[0:1, 0:1]), 1)
                            v.wait_ge(A("otr").h, j + 1)
                            for ri in range(2):
                                srcv = ps_o[:, ri, :].rearrange("p (a c) -> p a c", c=128)
                                dstv = oT[ri][:, :, j * 128:(j + 1) * 128]
                                ins = v.tensor_copy(dstv, srcv)
                            A("oev").inc(ins, 2)

                    @block.scalar
                    def _(sc):
                        sch = sems["a_sch"]
                        def SC(ins):
                            n = sch.inc(ins, 1)
                            sc.wait_ge(sch.h, n)
                        for j in range(NTT):
                            sc.wait_ge(A("mg").h, 2 * j + 1)
                            if j >= 1:
                                sc.wait_ge(A("mg").h, 2 * j)
                            SC(sc.activation(out=phi[:], in_=trg[:], func=AF.Abs))
                            A("pev").inc(sc.activation(out=phi[:], in_=phi[:], func=AF.Sin,
                                                       bias=PI / 2, scale=-1.0), 1)
                            ins = sc.activation(out=trg[:], in_=trg[:], func=AF.Sin, scale=-1.0)
                            A("pev").inc(ins, 1)

                    @block.tensor
                    def _(t):
                        def otr(j):
                            t.wait_ge(A("dot").h, j + 1)
                            if j >= 2:
                                t.wait_ge(A("oev").h, 2 * (j - 1))
                            for ri in range(2):
                                for kt in range(2):
                                    ins = t.transpose(out=ps_o[:, ri, kt * 128:(kt + 1) * 128],
                                                      in_=obf[:, ri, kt * 128:(kt + 1) * 128],
                                                      identity=g["sb_id"][:])
                            A("otr").inc(ins, 1)
                        for j in range(NTT):
                            b = j % 2
                            if j >= 1:
                                otr(j - 1)
                            t.wait_ge(A("mg").h, 2 * (j + 1))
                            for c9 in range(NC9):
                                if NC9 * j + c9 >= 2:
                                    t.wait_ge(A("csev").h, NC9 * j + c9 - 1)
                                ins = t.matmul(ps_cs[:, c9 % 2, 0:CW[c9]], g["sb_tri"][:],
                                               merged[b][:, CO[c9]:CO[c9] + CW[c9]],
                                               start=True, stop=True)
                                A("cs").inc(ins, 1)
                        otr(NTT - 1)

        with contextlib.ExitStack() as ctx:
            w_out_bf = ctx.enter_context(nc.sbuf_tensor("a_woutb", [128, 6, D], BF16))
            sg_sb = ctx.enter_context(nc.sbuf_tensor("a_sg", [128, 6, 512], BF16))
            gT = ctx.enter_context(nc.sbuf_tensor("a_gT", [128, 6, 512], BF16))
            xo = [ctx.enter_context(nc.sbuf_tensor(f"a_xo{i}", [128, D], F32)) for i in range(2)]
            xn = [ctx.enter_context(nc.sbuf_tensor(f"a_xn{i}", [128, D], F32)) for i in range(2)]
            ps_y = ctx.enter_context(nc.psum_tensor("a_psy", [128, 2, 512], F32))
            ps_x = ctx.enter_context(nc.psum_tensor("a_psx", [128, D], F32))

            yorder = list(range(6, 12)) + list(range(0, 6))
            ycons = []

            wout_n = [0]
            with nc.Block() as block:
                @block.gpsimd
                def _(gp):
                    wout_n[0] = A("wld").c[0].inc(gp.dma_start(
                        w_out_bf[:], g["a_wout"].rearrange("(a p) c -> p a c", p=128)), 16)
                    for j in range(NTT):
                        if j >= 2:
                            gp.wait_ge(A("xnew").h, j - 1)
                        A("x2").inc(j, gp.dma_start(xo[j % 2][:], src_x[j * 128:(j + 1) * 128, :]))
                        gp.wait_ge(A("xnew").h, j + 1)
                        A("xst").inc(j, gp.dma_start(dst_x[j * 128:(j + 1) * 128, :], xn[j % 2][:]))
                    A("xst").wait_all(gp)

                @block.vector
                def _(v):
                    for cq in range(4):
                        for mi, mb in enumerate(range(0, 6)):
                            i = cq * 12 + 6 + mi
                            v.wait_ge(A("ymm").h, i + 1)
                            v.wait_ge(A("ysg").h, cq * 6 + mb + 1)
                            if cq >= 1:
                                v.wait_ge(A("omm").h, 4 * cq)
                            ins = v.tensor_tensor(out=gT[:, mb, :], in0=ps_y[:, i % 2, :],
                                                  in1=sg_sb[:, mb, :], op=ALU.mult)
                            A("ygt").inc(ins, 1)
                        for tt in range(4):
                            j = cq * 4 + tt
                            bb = j % 2
                            v.wait_ge(A("omm").h, j + 1)
                            A("x2").wait(v, j)
                            if j >= 2:
                                A("xst").wait(v, j - 2)
                            ins = v.tensor_tensor(out=xn[bb][:], in0=ps_x[:], in1=xo[bb][:],
                                                  op=ALU.add)
                            A("xnew").inc(ins, 1)

                @block.scalar
                def _(sc):
                    for cq in range(4):
                        for gi in range(6):
                            i = cq * 12 + gi
                            sc.wait_ge(A("ymm").h, i + 1)
                            if cq >= 1:
                                sc.wait_ge(A("ygt").h, (cq - 1) * 6 + gi + 1)
                            ins = sc.activation(out=sg_sb[:, gi, :], in_=ps_y[:, i % 2, :],
                                                func=AF.Silu)
                            A("ysg").inc(ins, 1)

                @block.tensor
                def _(t):
                    for cq in range(4):
                        for idx, mb in enumerate(yorder):
                            i = cq * 12 + idx
                            if mb < 6:
                                kt = 1 if mb >= 3 else 0
                                col0 = mb * 128
                            else:
                                kt = 1 if mb >= 9 else 0
                                col0 = 768 + (mb - 6) * 128
                            if i >= 2:
                                s_, n_ = ycons[i - 2]
                                t.wait_ge(s_.h, n_)
                            for ri in range(2):
                                ins = t.matmul(ps_y[:, i % 2, :],
                                               wblk_bf[:, ri, kt, col0:col0 + 128],
                                               oT[ri][:, kt, cq * 512:(cq + 1) * 512],
                                               start=(ri == 0), stop=(ri == 1))
                            A("ymm").inc(ins, 1)
                            if mb >= 6:
                                ycons.append((A("ysg"), cq * 6 + (mb - 6) + 1))
                            else:
                                ycons.append((A("ygt"), cq * 6 + mb + 1))
                        for tt in range(4):
                            j = cq * 4 + tt
                            if j == 0:
                                t.wait_ge(A("wld").c[0].h, wout_n[0])
                            t.wait_ge(A("ygt").h, cq * 6 + 6)
                            if j >= 1:
                                t.wait_ge(A("xnew").h, j)
                            for cb in range(6):
                                for nn_ in range(4):
                                    ins = t.matmul(ps_x[:, nn_ * 512:(nn_ + 1) * 512],
                                                   gT[:, cb, tt * 128:(tt + 1) * 128],
                                                   w_out_bf[:, cb, nn_ * 512:(nn_ + 1) * 512],
                                                   start=(cb == 0), stop=(cb == 5))
                            A("omm").inc(ins, 1)


def _host_layouts(inputs):
    wre = np.asarray(inputs["L0_attn_W_re"])
    wim = np.asarray(inputs["L0_attn_W_im"])
    blk = np.zeros((2, 2, 128, 1536), dtype=np.float32)
    for idx, w in enumerate((wre, wim)):
        for k in range(K_):
            kt, khl = divmod(k * H_, 128)
            rows = slice(khl, khl + H_)
            blk[idx, kt, rows, k * 24:(k + 1) * 24] = w[k, :, 0:24]
            blk[idx, kt, rows, 768 + k * 24:768 + (k + 1) * 24] = w[k, :, 24:48]
    theta = np.asarray(inputs["L0_attn_theta"]).reshape(1, MEM * M_)
    norm = np.asarray(inputs["L0_attn_norm_scale"]).reshape(1, MEM)
    slopes = np.concatenate([np.asarray(inputs["L0_attn_decay_slopes"]),
                             np.asarray(inputs["L0_attn_anchor_slopes"])]).reshape(K_, 1).astype(np.float32)
    t_ramp = np.arange(L, dtype=np.float32)
    ramp = np.zeros((K_, L), dtype=np.float32)
    ramp[0:28, :] = (L - 1) - t_ramp[None, :]
    ramp[28:32, :] = t_ramp[None, :]
    return {
        "a_wmem": np.ascontiguousarray(inputs["L0_attn_W_mem"], dtype=np.float32),
        "a_kcol": np.ascontiguousarray(np.asarray(inputs["L0_attn_conv_k"]).reshape(4, ZC)),
        "a_wq": np.ascontiguousarray(inputs["L0_attn_W_q"], dtype=np.float32),
        "a_theta": np.ascontiguousarray(np.broadcast_to(theta, (128, MEM * M_))),
        "a_slopes": slopes,
        "a_sscale": np.ascontiguousarray(
            np.asarray(inputs["L0_attn_score_scale"]).reshape(K_, 1), dtype=np.float32),
        "a_wblk": blk,
        "a_norm": np.ascontiguousarray(norm.reshape(2, 128).T),
        "a_wout": np.ascontiguousarray(inputs["L0_attn_W_out"], dtype=np.float32),
        "a_ramp": ramp,
    }


def kernel(**inputs):
    x = np.ascontiguousarray(np.asarray(inputs["L0_attn_x"], dtype=np.float32))
    B = x.shape[0]
    base = {
        "wq_gate": np.ascontiguousarray(inputs["L0_ffn_wq_gate"]),
        "s_gate": np.ascontiguousarray(inputs["L0_ffn_s_gate"]),
        "wq_up": np.ascontiguousarray(inputs["L0_ffn_wq_up"]),
        "s_up": np.ascontiguousarray(inputs["L0_ffn_s_up"]),
        "wq_down": np.ascontiguousarray(inputs["L0_ffn_wq_down"]),
        "s_down": np.ascontiguousarray(inputs["L0_ffn_s_down"]),
        **_host_layouts(inputs),
    }
    nc = build_nc()
    in_maps = [{"x": x[b], **base} for b in range(B)]
    res = run_bass_kernel_spmd(nc, in_maps, core_ids=list(range(B)))
    return np.stack([np.asarray(r["out"], dtype=np.float32) for r in res.results], axis=0)
